# version 03
# baseline (speedup 1.0000x reference)
import math
import numpy as np

import concourse.bass as bass
import concourse.mybir as mybir
from concourse.bass_utils import run_bass_kernel_spmd

F32 = mybir.dt.float32
AF = mybir.ActivationFunctionType
ALU = mybir.AluOpType

T_LOC = 2048
HALO = 640
D = 1024
HID = 2730
NJ = 22
HPAD = NJ * 128
LI = 0.8 - 0.6 * math.exp(-0.3 * 12)

DBG = {"stop": None}



def _hilo(a64):
    hi = a64.astype(np.float32)
    lo = (a64 - hi.astype(np.float64)).astype(np.float32)
    return np.stack([hi, lo], 0)


def _xpos_tables(g0):
    pos = np.arange(g0 - HALO, g0 + T_LOC, dtype=np.float64)
    half = np.arange(0, 64, 2, dtype=np.float64)
    inv_freq = 1.0 / 10000.0 ** (half / 64.0)
    f = pos[:, None] * inv_freq[None, :]
    f = np.concatenate([f, f], 1)
    base = (half + 0.4 * 64.0) / (1.4 * 64.0)
    power = (pos - 4096 // 2) / float(512 // 2)
    sc = base[None, :] ** power[:, None]
    sc = np.concatenate([sc, sc], 1)
    sgn = np.where(np.arange(64) < 32, -1.0, 1.0)[None, :]
    cos, sin = np.cos(f), np.sin(f)
    Aq = cos * sc * 8.0
    Bq = sgn * sin * sc * 8.0
    Ak = cos / sc
    Bk = sgn * sin / sc
    return [_hilo(t).reshape(2, 21, 128, 64) for t in (Aq, Bq, Ak, Bk)]


def _attn_tables(g0):
    pos = np.arange(g0, g0 + T_LOC, dtype=np.float64)
    inv_freq = 1.0 / 10000.0 ** (np.arange(0, 64, 2, dtype=np.float64) / 64.0)
    f = np.repeat(pos[:, None] * inv_freq[None, :], 2, axis=1)
    sgn = np.where(np.arange(64) % 2 == 0, -1.0, 1.0)[None, :]
    cos, sin = np.cos(f), np.sin(f)
    s4 = 64.0 ** (-0.25)
    return [_hilo(t).reshape(2, 16, 128, 64)
            for t in (cos * s4, sgn * sin * s4, cos, sgn * sin)]


def _fm(x):
    T = x.shape[0]
    return np.ascontiguousarray(x.reshape(T, 8, 128).transpose(2, 1, 0))


CONST_NAMES = ["ln_eps", "rms_eps", "li", "one_m_li", "inv_d",
               "inv128", "half", "one"]
CONST_VALS = [1e-5, 1e-8, LI, 1.0 - LI, 1.0 / 1024.0, 1.0 / 128.0, 0.5, 1.0]


def _prep_shared(inp):
    W = {}
    W["wqkv"] = np.ascontiguousarray(inp["L0_ffn_wqkv"].astype(np.float32))
    W["wo_w"] = np.ascontiguousarray(inp["L0_ffn_wo"].astype(np.float32))
    w1 = inp["L0_ffn_w1"].astype(np.float32)
    b1 = inp["L0_ffn_b1"].astype(np.float32)
    w1p = np.zeros((2, 1024, 2 * HPAD), np.float32)
    b1p = np.zeros((2, 2 * HPAD), np.float32)
    for j in range(NJ):
        a0 = 128 * j
        n = min(128, HID - a0)
        if n <= 0:
            continue
        w1p[:, :, 256 * j:256 * j + n] = w1[:, :, a0:a0 + n]
        w1p[:, :, 256 * j + 128:256 * j + 128 + n] = w1[:, :, HID + a0:HID + a0 + n]
        b1p[:, 256 * j:256 * j + n] = b1[:, a0:a0 + n]
        b1p[:, 256 * j + 128:256 * j + 128 + n] = b1[:, HID + a0:HID + a0 + n]
    W["w1p"] = w1p
    W["b1p"] = np.ascontiguousarray(b1p.reshape(2, 2 * NJ, 128).transpose(0, 2, 1))
    w2p = np.zeros((2, HPAD, 1024), np.float32)
    w2p[:, :HID] = inp["L0_ffn_w2"].astype(np.float32)
    W["w2p"] = w2p
    W["b2v"] = np.ascontiguousarray(
        inp["L0_ffn_b2"].astype(np.float32).reshape(2, 8, 128).transpose(0, 2, 1))
    W["lngv"] = np.ascontiguousarray(
        inp["L0_ffn_ln_g"].astype(np.float32).reshape(2, 8, 128).transpose(0, 2, 1))
    W["lnbv"] = np.ascontiguousarray(
        inp["L0_ffn_ln_b"].astype(np.float32).reshape(2, 8, 128).transpose(0, 2, 1))
    W["lnbc"] = np.ascontiguousarray(
        inp["L0_ffn_ln_b"].astype(np.float32).reshape(2, 8, 128).transpose(0, 2, 1))
    W["qsv"] = np.ascontiguousarray(inp["L0_ffn_q_scale"].astype(np.float32))
    W["ksv"] = np.ascontiguousarray(inp["L0_ffn_k_scale"].astype(np.float32))
    W["aW"] = np.stack([inp["L0_attn_Wq"], inp["L0_attn_Wk"],
                        inp["L0_attn_Wv"], inp["L0_attn_Wo"]]).astype(np.float32)
    W["alnw"] = inp["L0_attn_ln_w"].astype(np.float32).reshape(1, 128)
    W["alnb"] = inp["L0_attn_ln_b"].astype(np.float32).reshape(1, 128)
    W["lamv"] = np.ascontiguousarray(np.stack(
        [inp["L0_attn_lambda_q1"], inp["L0_attn_lambda_k1"],
         inp["L0_attn_lambda_q2"], inp["L0_attn_lambda_k2"]]).astype(np.float32))
    return W


def _make_core_inputs(c, x, W):
    seq, half = c // 2, c % 2
    g0 = half * T_LOC
    d = dict(W)
    xs = np.zeros((HALO + T_LOC, D), np.float32)
    lo = g0 - HALO
    src_lo = max(lo, 0)
    xs[src_lo - lo:] = x[seq, src_lo:g0 + T_LOC]
    d["xin"] = _fm(xs)
    xt = _xpos_tables(g0)
    d["xAq"], d["xBq"], d["xAk"], d["xBk"] = xt
    it = _attn_tables(g0)
    d["iAq"], d["iBq"], d["iA"], d["iB"] = it
    iu = np.arange(128)
    utri = np.where(iu[None, :] >= iu[:, None], 0.0, -30000.0).astype(np.float32)
    ltri = np.where(iu[None, :] <= iu[:, None], 0.0, -30000.0).astype(np.float32)
    d["masks"] = np.ascontiguousarray(np.stack([utri, ltri]))
    fl = np.zeros((128, 18), np.float32)
    if half == 1:
        fl[:, 0] = 1.0
    for tau in range(16):
        if half == 0 and tau < 4:
            fl[:, 2 + tau] = -30000.0
    d["flags"] = fl
    cv = np.zeros((128, len(CONST_VALS) * 2), np.float32)
    for i, v in enumerate(CONST_VALS):
        h = np.float32(v)
        cv[:, 2 * i] = h
        cv[:, 2 * i + 1] = np.float32(np.float64(v) - np.float64(h))
    d["consts"] = cv
    d["identm"] = np.eye(128, dtype=np.float32)
    return d



ENGINES = ("te", "ve", "sc", "sy", "gp")
ENG_SEM = {"te": "te", "ve": "ve", "sc": "sc", "sy": "sy", "gp": "gp"}


class Step:
    __slots__ = ("fn", "deps", "sem", "inc", "eng", "need", "val", "waits")

    def __init__(self, fn, deps, sem, inc, eng):
        self.fn, self.deps, self.sem, self.inc, self.eng = fn, deps, sem, inc, eng
        self.need = False
        self.val = None
        self.waits = ()


DMA_SLOTS = {"dsy": 12, "dgp": 6}


class Prog:
    def __init__(self):
        self.steps = {e: [] for e in ENGINES}
        self.slot_rr = {"dsy": 0, "dgp": 0}
        self.slot_prev = {}

    def emit(self, eng, fn, deps=(), sem=None, inc=None):
        if sem is None:
            sem = ENG_SEM[eng]
        deps = [d for d in deps if d is not None]
        if inc is None:
            inc = 16 if sem in ("dsy", "dgp") else 1
        if sem in ("dsy", "dgp"):
            q = sem
            slot = self.slot_rr[q]
            self.slot_rr[q] = (slot + 1) % DMA_SLOTS[q]
            sem = f"{q}{slot}"
            prev = self.slot_prev.get(sem)
            if prev is not None:
                deps = deps + [prev]
        st = Step(fn, deps, sem, inc, eng)
        if sem.startswith("dsy") or sem.startswith("dgp"):
            self.slot_prev[sem] = st
        self.steps[eng].append(st)
        return st

    def last(self, eng):
        return self.steps[eng][-1] if self.steps[eng] else None

    def barrier(self, engs=("te", "ve", "sc")):
        return [self.last(e) for e in engs if self.last(e) is not None]

    def sem_names(self):
        s = set()
        for e in ENGINES:
            for st in self.steps[e]:
                s.add(st.sem)
        return sorted(s)

    def finalize(self):
        for e in ENGINES:
            for st in self.steps[e]:
                if st.sem.startswith("dsy") or st.sem.startswith("dgp"):
                    st.need = True
                for dd in st.deps:
                    dd.need = True
        cnt = {}
        for e in ENGINES:
            for st in self.steps[e]:
                if st.need:
                    cnt[st.sem] = cnt.get(st.sem, 0) + st.inc
                    st.val = cnt[st.sem]
        for e in ENGINES:
            hwm = {}
            for st in self.steps[e]:
                waits = []
                for dd in st.deps:
                    if dd.val is None:
                        continue
                    if hwm.get(dd.sem, 0) >= dd.val:
                        continue
                    waits.append((dd.sem, dd.val))
                    hwm[dd.sem] = dd.val
                st.waits = waits



def build(nc, p):
    dt = nc.dram_tensor
    xin = dt("xin", [128, 8, HALO + T_LOC], F32, kind="ExternalInput").ap()
    wqkv_d = dt("wqkv", [2, 1024, 1536], F32, kind="ExternalInput").ap()
    wo_d = dt("wo_w", [2, 512, 1024], F32, kind="ExternalInput").ap()
    w1p_d = dt("w1p", [2, 1024, 2 * HPAD], F32, kind="ExternalInput").ap()
    b1p_d = dt("b1p", [2, 128, 2 * NJ], F32, kind="ExternalInput").ap()
    w2p_d = dt("w2p", [2, HPAD, 1024], F32, kind="ExternalInput").ap()
    b2v_d = dt("b2v", [2, 128, 8], F32, kind="ExternalInput").ap()
    lng_d = dt("lngv", [2, 128, 8], F32, kind="ExternalInput").ap()
    lnb_d = dt("lnbv", [2, 128, 8], F32, kind="ExternalInput").ap()
    lnbc_d = dt("lnbc", [2, 128, 8], F32, kind="ExternalInput").ap()
    qsv_d = dt("qsv", [2, 64], F32, kind="ExternalInput").ap()
    ksv_d = dt("ksv", [2, 64], F32, kind="ExternalInput").ap()
    aW_d = dt("aW", [4, 1024, 1024], F32, kind="ExternalInput").ap()
    alnw_d = dt("alnw", [1, 128], F32, kind="ExternalInput").ap()
    alnb_d = dt("alnb", [1, 128], F32, kind="ExternalInput").ap()
    lam_d = dt("lamv", [4, 64], F32, kind="ExternalInput").ap()
    tabs_d = {}
    for nm in ("xAq", "xBq", "xAk", "xBk"):
        tabs_d[nm] = dt(nm, [2, 21, 128, 64], F32, kind="ExternalInput").ap()
    for nm in ("iAq", "iBq", "iA", "iB"):
        tabs_d[nm] = dt(nm, [2, 16, 128, 64], F32, kind="ExternalInput").ap()
    masks_d = dt("masks", [2, 128, 128], F32, kind="ExternalInput").ap()
    flags_d = dt("flags", [128, 18], F32, kind="ExternalInput").ap()
    consts_d = dt("consts", [128, len(CONST_VALS) * 2], F32, kind="ExternalInput").ap()
    ident_d = dt("identm", [128, 128], F32, kind="ExternalInput").ap()
    out_d = dt("out", [128, 8, T_LOC], F32, kind="ExternalOutput").ap()

    xpost_d = dt("xpost", [128, 8, T_LOC], F32).ap()
    bounce_d = dt("bounce", [80, 2, 8, 128], F32).ap()
    xch_in = [dt(f"xci{i}", [128, 8, HALO], F32).ap() for i in range(3)]
    xch_out = [dt(f"xco{i}", [2, 128, 8, HALO], F32).ap() for i in range(3)]
    sch_in = [dt(f"sci{i}", [128, 8, 1], F32).ap() for i in range(4)]
    sch_out = [dt(f"sco{i}", [2, 128, 8, 1], F32).ap() for i in range(4)]
    dbg_d = dbg2_d = None
    if DBG["stop"]:
        dbg_d = dt("dbg", [128, 8, 2049], F32, kind="ExternalOutput").ap()
        dbg2_d = dt("dbg2", [128, 8, HALO], F32, kind="ExternalOutput").ap()

    ctx = []

    def sb(name, shape):
        t = nc.sbuf_tensor(name, shape, F32)
        ctx.append(t)
        return t.__enter__()

    def psa(name, shape):
        t = nc.psum_tensor(name, shape, F32)
        ctx.append(t)
        return t.__enter__()

    AR_N = 49160
    AR = sb("AR", [128, AR_N])

    def V(off, shape):
        free = shape[1:]
        n = 1
        for s in free:
            n *= s
        ap = AR[:, off:off + n]
        if len(free) <= 1:
            return ap
        pat = "p (" + " ".join(f"d{i}" for i in range(len(free))) + ") -> p " + \
            " ".join(f"d{i}" for i in range(len(free)))
        return ap.rearrange(pat, **{f"d{i}": s for i, s in enumerate(free)})

    class VT:
        def __init__(self, off, shape):
            self.off = off
            self.shape = shape

        def __getitem__(self, idx):
            return V(self.off, self.shape)[idx]

    U = VT(0, [128, 8, 2049])
    WQKV_O, KTR_O, VR_O, QTW_O, UH_O = 16392, 28680, 32776, 36872, 38920
    SUB_O = 44040
    wqkv = VT(WQKV_O, [128, 8, 1536])
    kTr = VT(KTR_O, [128, 4, 8, 128])
    vr = VT(VR_O, [128, 8, 512])
    qTw = VT(QTW_O, [128, 4, 4, 128])
    Uh = VT(UH_O, [128, 8, HALO])
    xtf = VT(SUB_O, [128, 4, 4, 64])
    wk = VT(SUB_O + 1024, [128, 3, 512])
    Pt = VT(SUB_O + 2560, [128, 2, 640])
    PTs = VT(SUB_O + 3840, [128, 640])
    wob = VT(SUB_O + 4480, [128, 2, 128])
    qsq = VT(SUB_O + 1024, [128, 512])
    oTs = VT(UH_O, [128, 4, 512])
    osb = VT(UH_O + 2048, [128, 4, 512])
    w1b = VT(UH_O, [128, 2, 8, 256])
    w2b = VT(QTW_O, [128, 2, 8, 128])
    s2n = VT(SUB_O, [128, 8, 256])
    h2 = VT(SUB_O + 2048, [128, 2, 256])
    gel = VT(SUB_O + 2560, [128, 2, 256])
    halo_a = VT(SUB_O, [128, 8, HALO])
    xslab_u = VT(KTR_O, [128, 8, 512])
    xslab_x = VT(UH_O, [128, 8, 512])
    xq_x = VT(SUB_O, [128, 8, 512])
    aq = VT(0, [128, 4, 1024])
    ak = VT(4096, [128, 4, 1024])
    av = VT(8192, [128, 4, 1024])
    aWb = VT(12288, [128, 2, 8, 512])
    ascr = VT(20480, [128, 4096])
    aTs = VT(24576, [128, 8, 512])
    aout = VT(28672, [128, 1024])
    itb = VT(SUB_O, [128, 4, 16, 64])
    awk = VT(29696, [128, 2, 512])
    xslab_a = VT(34824, [128, 8, 512])
    xq_a = VT(24576, [128, 8, 512])

    ident = sb("ident", [128, 128])
    maskU = sb("maskU", [128, 128])
    maskL = sb("maskL", [128, 128])
    flags = sb("flagsS", [128, 18])
    consts = sb("constsS", [128, len(CONST_VALS)])
    consts_raw = sb("constsR", [128, len(CONST_VALS) * 2])
    onescol = sb("onescol", [128, 1])
    onesrow = sb("onesrow", [1, 128])
    lamneg = sb("lamneg", [128, 1])
    lnw_rep = sb("lnwrep", [128, 128])
    lnb_rep = sb("lnbrep", [128, 128])
    b1s = sb("b1s", [128, 2 * NJ])
    b2s = sb("b2s", [128, 8])
    lngs = sb("lngs", [128, 8])
    lnbcol = sb("lnbcol", [128, 8])
    qs_rep = sb("qsrep", [128, 64])
    qsp_rep = sb("qsprep", [128, 64])
    ks_rep = sb("ksrep", [128, 64])
    ksp_rep = sb("ksprep", [128, 64])
    c2s = sb("c2s", [128, 2 * NJ])
    lamt = sb("lamt", [1, 256])
    lamr = sb("lamr", [1, 8])
    rnq = sb("rnq", [128, 4, 8])
    rnqn = sb("rnqn", [128, 4, 8])
    rnk = sb("rnk", [128, 8])
    rcp = sb("rcp", [128, 8])
    pend = sb("pend", [128, 8, 1])
    mpend = sb("mpend", [128, 8, 8])
    mx = sb("mx", [128, 8])
    tmps = sb("tmps", [128, 8, 128])
    stat1 = sb("stat1", [128, 4])
    stat2 = sb("stat2", [128, 4])
    statmu = sb("statmu", [128, 4])
    statr = sb("statr", [128, 4])
    statpk = sb("statpk", [128, 8])
    statT = sb("statT", [8, 128])
    bcast = VT(VR_O, [128, 2, 4, 128])
    sqT = VT(VR_O + 1024, [128, 1024])
    aS1 = sb("aS1", [128, 64])
    aS2 = sb("aS2", [128, 64])
    aw1 = sb("aw1", [128, 64])
    aw2 = sb("aw2", [128, 64])
    awm = sb("awm", [128, 8])
    awsum = sb("awsum", [128, 8])
    ss8 = sb("ss8", [128, 8])
    aWob = sb("aWob", [128, 2, 128])
    sxa = sb("sxa", [128, 8, 1])
    sxb = sb("sxb", [128, 8, 1])
    sxc = sb("sxc", [128, 8, 1])

    P0 = psa("P0", [128, 1024])
    P1 = psa("P1", [128, 1024])
    P2 = psa("P2", [128, 1024])
    P3 = psa("P3", [128, 1024])

    E = p.emit

    def dma(q, out, in_, deps=()):
        return E(q, lambda e, o=out, i=in_: e.dma_start(out=o, in_=i), deps,
                 sem="dsy" if q == "sy" else "dgp")

    def mm(out, lhsT, rhs, start, stop, deps=()):
        return E("te", lambda e, o=out, l=lhsT, r=rhs, s0=start, s1=stop:
                 e.matmul(o, l, r, start=s0, stop=s1, skip_group_check=True), deps)

    def tr(out, in_, deps=()):
        return E("te", lambda e, o=out, i=in_: e.transpose(o, i, ident[:]), deps)

    def tt(eng, out, a, b, op, deps=()):
        return E(eng, lambda e, o=out, x=a, y=b, z=op: e.tensor_tensor(o, x, y, z), deps)

    def tss(eng, out, a, s1, op, deps=()):
        return E(eng, lambda e, o=out, x=a, s=s1, z=op:
                 e.tensor_single_scalar(o, x, s, z), deps)

    def stt(eng, out, in0, scalar, in1, op0, op1, deps=()):
        return E(eng, lambda e, o=out, x=in0, s=scalar, y=in1, za=op0, zb=op1:
                 e.scalar_tensor_tensor(o, x, s, y, za, zb), deps)

    def act(out, in_, func, deps=(), bias=0.0, scale=1.0):
        return E("sc", lambda e, o=out, i=in_, f=func, b=bias, s=scale:
                 e.activation(out=o, in_=i, func=f, bias=b, scale=s), deps)

    def red(eng, out, in_, op, deps=(), negate=False):
        return E(eng, lambda e, o=out, i=in_, z=op, n=negate:
                 e.tensor_reduce(o, i, axis=mybir.AxisListType.X, op=z, negate=n), deps)

    def recip(eng, out, in_, deps=()):
        return E(eng, lambda e, o=out, i=in_: e.reciprocal(o, i), deps)

    def cp(eng, out, in_, deps=()):
        if eng == "sc":
            return E(eng, lambda e, o=out, i=in_: e.copy(out=o, in_=i), deps)
        return E(eng, lambda e, o=out, i=in_: e.tensor_copy(o, i), deps)

    def mset(eng, out, v, deps=()):
        return E(eng, lambda e, o=out, vv=v: e.memset(o, vv), deps)

    def AP(base, off_extra, dims):
        return bass.AP(base.tensor, base.offset + off_extra, dims)

    def cc_(name):
        i = CONST_NAMES.index(name)
        return consts[:, i:i + 1]

    ev_id = dma("sy", ident[:], ident_d)
    ev_mU = dma("sy", maskU[:], masks_d[0])
    ev_mL = dma("sy", maskL[:], masks_d[1])
    ev_fl = dma("sy", flags[:], flags_d)
    ev_cr = dma("sy", consts_raw[:], consts_d)
    nc_ = len(CONST_VALS)
    ev_c = tt("ve", consts[:],
              AP(consts_raw[:], 0, [consts_raw[:].ap[0], [2, nc_]]),
              AP(consts_raw[:], 1, [consts_raw[:].ap[0], [2, nc_]]),
              ALU.add, [ev_cr])
    ev_ones = mset("ve", onescol[:], 1.0)
    ev_onesr = mset("ve", onesrow[:], 1.0)
    ev_stT = mset("ve", statT[:], 0.0)
    ev_lam = dma("sy", lamt[:], AP(lam_d, 0, [[256, 1], [1, 256]]))
    e1 = tt("ve", lamr[0:1, 0:1].rearrange("a b -> a b") if False else lamt[0:1, 0:64],
            lamt[0:1, 0:64], lamt[0:1, 64:128], ALU.mult, [ev_lam])
    e2 = tt("ve", lamt[0:1, 64:128], lamt[0:1, 128:192], lamt[0:1, 192:256],
            ALU.mult, [ev_lam, e1])
    e3 = red("ve", lamr[0:1, 0:1], lamt[0:1, 0:64], ALU.add, [e1, e2])
    e4 = red("ve", lamr[0:1, 1:2], lamt[0:1, 64:128], ALU.add, [e2])
    e5 = act(lamr[0:1, 2:4], lamr[0:1, 0:2], AF.Exp, [e3, e4])
    e6 = tt("ve", lamr[0:1, 4:5], lamr[0:1, 3:4], lamr[0:1, 2:3], ALU.subtract, [e5])
    e7 = tss("ve", lamr[0:1, 4:5], lamr[0:1, 4:5],
             consts[0:1, CONST_NAMES.index("li"):CONST_NAMES.index("li") + 1],
             ALU.subtract, [e6, ev_c])
    e8 = mm(P3[:, 0:1], onesrow[:], lamr[0:1, 4:5], True, True, [e7, ev_onesr])
    ev_lamn = cp("ve", lamneg[:], P3[:, 0:1], [e8])

    ev_lnw = dma("sy", lnw_rep[:], AP(alnw_d, 0, [[0, 128], [1, 128]]))
    ev_lnb = dma("sy", lnb_rep[:], AP(alnb_d, 0, [[0, 128], [1, 128]]))
    ev_lnw = tss("ve", lnw_rep[:], lnw_rep[:], cc_("one_m_li"), ALU.mult,
                 [ev_lnw, ev_c])
    ev_lnb = tss("ve", lnb_rep[:], lnb_rep[:], cc_("one_m_li"), ALU.mult,
                 [ev_lnb, ev_c])

    def dma_acc(q, out, in_, deps=()):
        return E(q, lambda e, o=out, i=in_:
                 e.dma_start(out=o, in_=i, accum_op=ALU.add), deps,
                 sem="dsy" if q == "sy" else "dgp")

    bounce_ctr = [0]

    def ln_grp(grp, dstf):
        ng = len(grp)
        bar_ls = p.barrier(("ve", "sc"))
        evs = []
        prevk = bar_ls
        for k, (srcf, dps) in enumerate(grp):
            evt = [tr(P3[:, 128 * c:128 * c + 128], srcf(c),
                      dps + [ev_id] + (prevk if c == 0 else []))
                   for c in range(8)]
            e1_ = red("ve", stat1[:, k:k + 1], P3[:], ALU.add, evt)
            esq = act(sqT[:], P3[:], AF.Square, evt + [e1_])
            e2_ = red("ve", stat2[:, k:k + 1], sqT[:], ALU.add, [esq])
            evs += [e1_, e2_, esq]
            prevk = [esq, e1_, e2_]
        em = tss("ve", statmu[:, 0:ng], stat1[:, 0:ng], cc_("inv_d"), ALU.mult,
                 evs + [ev_c])
        ev = tss("ve", statr[:, 0:ng], stat2[:, 0:ng], cc_("inv_d"), ALU.mult, evs)
        et = tt("ve", statpk[:, 0:ng], statmu[:, 0:ng], statmu[:, 0:ng], ALU.mult, [em])
        ev = tt("ve", statr[:, 0:ng], statr[:, 0:ng], statpk[:, 0:ng],
                ALU.subtract, [ev, et])
        ev = tss("ve", statr[:, 0:ng], statr[:, 0:ng], cc_("ln_eps"), ALU.add, [ev])
        es = act(statr[:, 0:ng], statr[:, 0:ng], AF.Sqrt, [ev])
        ev = recip("ve", statr[:, 0:ng], statr[:, 0:ng], [es])
        ea = cp("ve", statpk[:, 0:ng], statmu[:, 0:ng], [em, et])
        eb = cp("ve", statpk[:, ng:2 * ng], statr[:, 0:ng], [ev, ea])
        ett = tr(P2[0:2 * ng, 0:128], statpk[:, 0:2 * ng], [eb])
        ecc = cp("sc", statT[0:2 * ng, :], P2[0:2 * ng, 0:128], [ett, ev_stT])
        slot = bounce_ctr[0] % 80
        bounce_ctr[0] += 1
        bv = bounce_d[slot]
        eo1 = dma("gp", bv[0, 0:ng, :], statT[0:ng, :], [ecc])
        eo2 = dma("gp", bv[1, 0:ng, :], statT[ng:2 * ng, :], [ecc])
        ein = dma("sy", bcast[:, :, 0:ng, :],
                  AP(bv, 0, [[0, 128], [8 * 128, 2], [128, ng], [1, 128]]),
                  [eo1, eo2] + bar_ls)
        res = []
        for k in range(ng):
            res.append(dstf(k, ein, bcast[:, 0, k, :], bcast[:, 1, k, :]))
        return res

    def u0_pass(src_main_d, main_off, halo_deps, deps):
        all_evs = []
        prev_apply = list(deps)
        for g in range(4):
            t0 = 512 * g
            eld = dma("sy", xslab_u[:],
                      src_main_d[:, :, main_off + t0:main_off + t0 + 512],
                      prev_apply)
            grp = []
            for k in range(4):
                grp.append((lambda c, k=k: xslab_u[:, c, 128 * k:128 * k + 128],
                            [eld]))

            def dstf(k, ein, mu_ap, r_ap, t0=t0):
                last = None
                for c in range(8):
                    sv = xslab_u[:, c, 128 * k:128 * k + 128]
                    ee1 = tt("ve", tmps[:, c, 0:128], sv, mu_ap, ALU.subtract, [ein])
                    last = tt("ve", U[:, c, 1 + t0 + 128 * k:1 + t0 + 128 * k + 128],
                              tmps[:, c, 0:128], r_ap, ALU.mult, [ee1])
                return last
            res = ln_grp(grp, dstf)
            prev_apply = res
            all_evs += res

        def hsrc(c, k):
            return Uh[:, c, 128 * k:128 * k + 128]

        def dsth(k_in_grp, ein, mu_ap, r_ap, koff=0):
            k = k_in_grp + koff
            last = None
            for c in range(8):
                ee1 = tt("ve", tmps[:, c, 0:128], hsrc(c, k), mu_ap,
                         ALU.subtract, [ein])
                last = tt("ve", Uh[:, c, 128 * k:128 * k + 128], tmps[:, c, 0:128],
                          r_ap, ALU.mult, [ee1])
            return last
        grp = [(lambda c, k=k: hsrc(c, k), list(halo_deps)) for k in range(4)]
        r1 = ln_grp(grp, dsth)
        r2 = ln_grp([(lambda c: hsrc(c, 4), list(halo_deps) + r1)],
                    lambda kk, ein, m, r: dsth(kk, ein, m, r, koff=4))
        all_evs += r1 + r2
        ehc = cp("ve", U[:, :, 0:1], Uh[:, :, HALO - 1:HALO], r1 + r2)
        all_evs.append(ehc)
        return all_evs

    def exchange_big(idx, src_ap, src_deps, dst_deps):
        e1 = dma("gp", xch_in[idx], src_ap, src_deps)
        e2 = E("gp", lambda e, i=idx: e.collective_compute(
            "AllGather", ALU.bypass,
            replica_groups=[[0, 1], [2, 3], [4, 5], [6, 7]],
            ins=[xch_in[i][:]], outs=[xch_out[i][:]]), [e1], sem="cc", inc=1)
        e3 = dma("gp", Uh[:], xch_out[idx][0], [e2] + dst_deps)
        e4 = dma("gp", halo_a[:], xch_out[idx][1], [e2] + dst_deps)
        e5 = tss("ve", Uh[:], Uh[:], flags[:, 0:1], ALU.mult, [e3, ev_fl])
        e6 = stt("ve", Uh[:], halo_a[:], flags[:, 1:2], Uh[:], ALU.mult, ALU.add,
                 [e4, e5])
        return e6

    def exchange_small(idx, src_ap, src_deps, dst_ap, dst_deps):
        e0 = cp("ve", sxc[:], src_ap, src_deps)
        e1 = dma("gp", sch_in[idx], sxc[:], [e0])
        e2 = E("gp", lambda e, i=idx: e.collective_compute(
            "AllGather", ALU.bypass,
            replica_groups=[[0, 1], [2, 3], [4, 5], [6, 7]],
            ins=[sch_in[i][:]], outs=[sch_out[i][:]]), [e1], sem="cc", inc=1)
        e3 = dma("gp", sxa[:], sch_out[idx][0], [e2] + dst_deps)
        e4 = dma("gp", sxb[:], sch_out[idx][1], [e2])
        e5 = tss("ve", dst_ap, sxa[:], flags[:, 0:1], ALU.mult,
                 [e3, ev_fl] + dst_deps)
        e6 = stt("ve", dst_ap, sxb[:], flags[:, 1:2], dst_ap,
                 ALU.mult, ALU.add, [e4, e5])
        return e6

    def s_view(c, tau):
        base = 1 + 128 * tau - (1 if c >= 4 else 0)
        return U[:, c, base:base + 128]

    def sh_view(c, hi):
        base = HALO - 512 + 128 * hi - (1 if c >= 4 else 0)
        return Uh[:, c, base:base + 128]

    def load_scales(l):
        e1 = dma("sy", qs_rep[:], AP(qsv_d, 64 * l, [[0, 128], [1, 64]]))
        e1b = dma("sy", qsp_rep[:], AP(qsv_d, 64 * l + 32,
                                       [[0, 128], [-32, 2], [1, 32]]), [e1])
        e2 = dma("sy", ks_rep[:], AP(ksv_d, 64 * l, [[0, 128], [1, 64]]), [e1b])
        e2b = dma("sy", ksp_rep[:], AP(ksv_d, 64 * l + 32,
                                       [[0, 128], [-32, 2], [1, 32]]), [e2])
        return e2b

    def load_xpos_win(t0_tile, ev_scales, deps):
        ti0 = t0_tile + 5
        vbar = p.barrier(("ve", "te"))
        out = []
        for ti, nm in enumerate(("xAq", "xBq", "xAk", "xBk")):
            src_hi = AP(tabs_d[nm], ti0 * 128 * 64,
                        [[64, 128], [128 * 64, 4], [1, 64]])
            src_lo = AP(tabs_d[nm], 21 * 128 * 64 + ti0 * 128 * 64,
                        [[64, 128], [128 * 64, 4], [1, 64]])
            e0 = dma("gp", xtf[:, ti], src_hi, list(deps) + vbar)
            e1 = dma_acc("gp", xtf[:, ti], src_lo, [e0])
            fold = (qs_rep, qsp_rep, ks_rep, ksp_rep)[ti]
            fb = AP(fold[:], 0, [fold[:].ap[0], [0, 4], [1, 64]])
            out.append(tt("ve", xtf[:, ti], xtf[:, ti], fb, ALU.mult,
                          [e1, ev_scales]))
        return out

    qsq_guard = [[]]
    wk_guard = [[]]
    wo_state = {"ld": None, "g": {0: [], 1: []}}
    mlp_state = {"w1ld": None, "w2ld": None, "g": {0: [], 1: []}}
    aW_state = {"ld": None, "g": {0: [], 1: []}}
    awo_state = {"ld": None, "g": {0: [], 1: []}}

    def rotary(kind, src_ps, tau_in_win, tab_evs, deps):
        ti_A = 0 if kind == "q" else 2
        esq = act(qsq[:], src_ps, AF.Square, deps + qsq_guard[0])
        rn_dst = rnq[:, tau_in_win, :] if kind == "q" else rnk[:]
        er = red("ve", rn_dst, qsq[:].rearrange("p (h d) -> p h d", h=8),
                 ALU.add, [esq])
        qsq_guard[0] = [er]
        deps = list(deps) + [esq]
        er = tss("ve", rn_dst, rn_dst, 1e-24, ALU.max, [er])
        er = act(rn_dst, rn_dst, AF.Sqrt, [er])
        er = recip("ve", rn_dst, rn_dst, [er])
        Ai = xtf[:, ti_A, tau_in_win, :]
        Bi = xtf[:, ti_A + 1, tau_in_win, :]
        x3 = src_ps.rearrange("p (h d) -> p h d", h=8)
        Ab = AP(Ai, 0, [Ai.ap[0], [0, 8], [1, 64]])
        perm = AP(x3, 32, [x3.ap[0], x3.ap[1], [-32, 2], [1, 32]])
        Bb = AP(Bi, 0, [Bi.ap[0], [0, 8], [32, 2], [1, 32]])
        w0 = wk[:, 0, 0:512].rearrange("p (h d) -> p h d", h=8)
        w1_ = wk[:, 1, 0:512].rearrange("p (h s d) -> p h s d", h=8, s=2)
        e1 = tt("ve", w0, x3, Ab, ALU.mult, deps + tab_evs)
        e2 = tt("ve", w1_, perm, Bb, ALU.mult, deps + tab_evs)
        e3 = tt("ve", wk[:, 2, 0:512], wk[:, 0, 0:512], wk[:, 1, 0:512], ALU.add,
                [e1, e2] + wk_guard[0])
        return e3, er

    def emit_qkv_tau(l, tau_g, srcf, base_deps, want_q, tab_evs, bank_guards):
        slot = tau_g % 8
        piw = tau_g % 4
        res = {}
        kinds = (["k", "v", "q"] if want_q else ["k", "v"])
        for kind in kinds:
            dst = {"q": P3[:, 0:512], "k": P3[:, 512:1024], "v": P0[:, 0:512]}[kind]
            col0 = {"q": 0, "k": 512, "v": 1024}[kind]
            evm = None
            for c in range(8):
                evm = mm(dst, srcf(c), wqkv[:, c, col0:col0 + 512], c == 0, c == 7,
                         base_deps + (bank_guards if c == 0 else []))
            if kind == "v":
                res["v"] = cp("sc", vr[:, slot, :], dst, [evm])
            elif kind == "k":
                e3, er = rotary("k", dst, piw, tab_evs, [evm])
                ek = []
                for h in range(8):
                    ek.append(tss("ve", wk[:, 2, 64 * h:64 * h + 64],
                                  wk[:, 2, 64 * h:64 * h + 64], rnk[:, h:h + 1],
                                  ALU.mult, [e3, er]))
                evt = [tr(P1[:, 128 * cc:128 * cc + 128],
                          wk[:, 2, 128 * cc:128 * cc + 128], ek + bank_guards)
                       for cc in range(4)]
                res["k"] = cp("sc", kTr[:, :, slot, :],
                              P1[:, 0:512].rearrange("p (c t) -> p c t", c=4), evt)
                wk_guard[0] = [res["k"]]
            else:
                e3, er = rotary("q", dst, piw, tab_evs, [evm])
                res["q_rn"] = tss("ve", rnqn[:, piw, :], rnq[:, piw, :], -1.0,
                                  ALU.mult, [er])
                evt = [tr(P1[:, 128 * cc:128 * cc + 128],
                          wk[:, 2, 128 * cc:128 * cc + 128],
                          [e3, res.get("k")] + bank_guards)
                       for cc in range(4)]
                res["q"] = cp("sc", qTw[:, :, piw, :],
                              P1[:, 0:512].rearrange("p (c t) -> p c t", c=4), evt)
                wk_guard[0] = [res["q"]]
        return res

    def attn_qk(tau, h, kslots, Sp, deps):
        piw = tau % 4
        c, hf = h // 2, h % 2
        lq = qTw[64 * hf:64 * hf + 64, c, piw, :]
        evs = []
        for j, sl in enumerate(kslots):
            rk = kTr[64 * hf:64 * hf + 64, c, sl, :]
            evs.append(mm(Sp[:, 128 * j:128 * j + 128], lq, rk,
                          j == 0 or j == 4, False, deps))
        em0 = mm(Sp[:, 0:128], ident[:], maskU[:], False, True,
                 [evs[0], ev_mU, ev_id])
        em4 = mm(Sp[:, 512:640], ident[:], maskL[:, 0:128], False, True,
                 [evs[4], ev_mL])
        return evs + [em0, em4]

    def attn_soft(tau, h, Sp, qk_evs, pt_sel):
        piw = tau % 4
        prevN = 128 * (4 - piw)
        off_col = flags[:, 2 + tau:3 + tau]
        m0 = mx[:, 4 * pt_sel + 0:4 * pt_sel + 1]
        m1 = mx[:, 4 * pt_sel + 1:4 * pt_sel + 2]
        m2 = mx[:, 4 * pt_sel + 2:4 * pt_sel + 3]
        emx = red("ve", m0, Sp, ALU.max, qk_evs)
        ebp = stt("ve", m1, m0, rnqn[:, piw, h:h + 1], off_col,
                  ALU.mult, ALU.add, [emx, ev_fl])
        ebo = tss("ve", m2, m0, rnqn[:, piw, h:h + 1], ALU.mult, [emx])
        Pb = Pt[:, pt_sel, :]
        ee1 = E("sc", lambda e, o=Pb[:, 0:prevN], i=Sp[:, 0:prevN],
                b=m1, s=rnq[:, piw, h:h + 1]:
                e.activation(out=o, in_=i, func=AF.Exp, bias=b, scale=s), [ebp])
        ee2 = E("sc", lambda e, o=Pb[:, prevN:640], i=Sp[:, prevN:640],
                b=m2, s=rnq[:, piw, h:h + 1]:
                e.activation(out=o, in_=i, func=AF.Exp, bias=b, scale=s), [ebo])
        return [ee1, ee2]

    def attn_ptav(tau, h, kslots, pt_sel, exp_evs, pt_guard, o_start, o_guard):
        Pb = Pt[:, pt_sel, :]
        evt = [tr(P2[:, 128 * j:128 * j + 128], Pb[:, 128 * j:128 * j + 128],
                  exp_evs + pt_guard) for j in range(5)]
        epts = cp("sc", PTs[:], P2[:, 0:640], evt)
        eav = []
        for j, sl in enumerate(kslots):
            lp = PTs[:, 128 * j:128 * j + 128]
            gg = o_guard if j == 0 else []
            eav.append(mm(P3[:, 64 * h:64 * h + 64], lp,
                          vr[:, sl, 64 * h:64 * h + 64],
                          o_start and j == 0, h == 7 and j == 4, [epts] + gg))
            eav.append(mm(P3[:, 512 + h:513 + h], lp, onescol[:],
                          o_start and j == 0, h == 7 and j == 4,
                          [epts, ev_ones] + gg))
        return eav, epts

    def ffn_layer(l, lpos, ev_wq, u_deps, uh_deps):
        ev_sc = load_scales(l)
        ecol = cp("ve", U[:, :, 0:1], Uh[:, :, HALO - 1:HALO], uh_deps)
        u_deps = list(u_deps) + [ecol]
        wo_state["g"] = {0: wo_state["g"][0] + list(uh_deps),
                         1: wo_state["g"][1] + list(uh_deps)}
        tab_evs = load_xpos_win(-4, ev_sc, list(uh_deps))
        kv_prev = []
        bank_g = p.barrier(("ve", "sc"))
        for hi in range(4):
            rr = emit_qkv_tau(l, hi - 4, lambda c, hi=hi: sh_view(c, hi),
                              uh_deps + [ev_wq] + tab_evs, False, tab_evs, bank_g)
            kv_prev += [rr["k"], rr["v"]]
            bank_g = p.barrier(("ve", "sc"))

        pend_ev = None
        u_evs = []
        for w in range(4):
            tab_evs = load_xpos_win(4 * w, ev_sc, tab_evs)
            cur_guard = p.barrier(("ve", "sc"))
            kv_ev = []
            q_ev = []
            for i in range(4):
                tau = 4 * w + i
                dd = u_deps + [ev_wq] + tab_evs
                if pend_ev is not None and i == 0:
                    dd = dd + [pend_ev]
                rr = emit_qkv_tau(l, tau, lambda c, tau=tau: s_view(c, tau),
                                  dd, True, tab_evs, cur_guard)
                kv_ev += [rr["k"], rr["v"]]
                q_ev.append((rr["q"], rr["q_rn"]))
                cur_guard = p.barrier(("ve", "sc"))
            if pend_ev is not None:
                eu = tt("ve", U[:, :, 512 * w:512 * w + 1],
                        U[:, :, 512 * w:512 * w + 1], pend[:, :, 0:1], ALU.add,
                        [pend_ev] + kv_ev[-2:] + [q_ev[0][0]])
                u_evs.append(eu)
                pend_ev = None

            pairs = [(i, h) for i in range(4) for h in range(8)]
            state = {}
            qkv_done = kv_ev + [q for q, _ in q_ev] + [r for _, r in q_ev]
            o_evac = []

            def flush(idx_f, o_evac=o_evac, state=state):
                s = state[idx_f]
                og = [o_evac[-1]] if (o_evac and s["h"] == 0) else []
                eav, epts = attn_ptav(s["tau"], s["h"], s["kslots"],
                                      idx_f % 2, s["exp"],
                                      [state[idx_f - 2]["epts"]] if idx_f >= 2 else [],
                                      s["h"] == 0, og)
                s["av"] = eav
                s["epts"] = epts
                if s["h"] == 7:
                    erc = recip("ve", rcp[:], P3[:, 512:520], eav)
                    for hh in range(8):
                        o_evac.append(tss("ve",
                                          osb[:, s["i"], 64 * hh:64 * hh + 64],
                                          P3[:, 64 * hh:64 * hh + 64],
                                          rcp[:, hh:hh + 1], ALU.mult, [erc]))

            for idx, (i, h) in enumerate(pairs):
                tau = 4 * w + i
                kslots = [(4 * (w - 1) + b) % 8 for b in range(i, 4)] + \
                         [(4 * w + b) % 8 for b in range(0, i + 1)]
                Sp = (P0, P1)[idx % 2][:, 0:640]
                deps = qkv_done + kv_prev
                if idx >= 2:
                    deps = deps + state[idx - 2]["exp"]
                qk = attn_qk(tau, h, kslots, Sp, deps)
                ex = attn_soft(tau, h, Sp, qk, idx % 2)
                state[idx] = {"exp": ex, "kslots": kslots, "tau": tau, "h": h,
                              "i": i}
                if idx >= 1:
                    flush(idx - 1)
            flush(len(pairs) - 1)
            kv_prev = kv_ev

            bar = p.barrier(("ve", "sc"))
            evtr = []
            for i in range(4):
                for oc in range(4):
                    dd = o_evac + (bar if (i == 0 and oc == 0) else []) + \
                        (evtr[-1:] if evtr else [])
                    et = tr(P0[:, 0:128], osb[:, i, 128 * oc:128 * oc + 128], dd)
                    evtr.append(cp("sc", oTs[:, oc, 128 * i:128 * i + 128],
                                   P0[:, 0:128], [et]))
            bank_ev = {}
            base = 1 + 512 * w
            for dc in range(8):
                pps = (P1, P2)[(dc // 2) % 2][:, 512 * (dc % 2):512 * (dc % 2) + 512]
                evm = None
                for oc in range(4):
                    buf = wob[:, oc % 2, :]
                    ld = dma("sy", buf, wo_d[l, 128 * oc:128 * oc + 128,
                                             128 * dc:128 * dc + 128],
                             ([wo_state["ld"]] if wo_state["ld"] else []) +
                             wo_state["g"][oc % 2])
                    wo_state["ld"] = ld
                    dd = [ld] + evtr
                    if oc == 0 and dc - 4 in bank_ev:
                        dd.append(bank_ev[dc - 4])
                    if oc == 0 and dc < 4:
                        dd += bar
                    evm = mm(pps, buf, oTs[:, oc, :], oc == 0, oc == 3, dd)
                    wo_state["g"][oc % 2] = [evm]
                ep = cp("ve", pend[:, dc, 0:1], pps[:, 511:512], [evm])
                eu = tt("ve", U[:, dc, base:base + 511], U[:, dc, base:base + 511],
                        pps[:, 0:511], ALU.add, [evm, ep])
                bank_ev[dc] = eu
                u_evs.append(eu)
                pend_ev = eu
        eu = tt("ve", U[:, :, 2048:2049], U[:, :, 2048:2049], pend[:, :, 0:1],
                ALU.add, [pend_ev])
        u_evs.append(eu)
        attn_done = p.barrier()
        if DBG["stop"] == "l1a" and lpos == 0:
            e1 = dma("gp", dbg_d, U[:], u_evs + attn_done)
            E("gp", lambda e: e.memset(onescol[0:1, 0:1], 1.0), [e1])
            return None

        exch = exchange_small(lpos, U[:, :, 2048:2049], [eu], U[:, :, 0:1],
                              attn_done)
        if DBG["stop"] == "l1x" and lpos == 0:
            e1 = dma("gp", dbg_d, U[:], u_evs + attn_done + [exch])
            E("gp", lambda e: e.memset(onescol[0:1, 0:1], 1.0), [e1])
            return None
        eb1 = dma("sy", b1s[:], b1p_d[l])
        eb2 = dma("sy", b2s[:], b2v_d[l])
        elg = dma("sy", lngs[:], lng_d[l])
        elbc = dma("sy", lnbcol[:], lnbc_d[l])

        mlp_out = []
        c2_ev = [None] * (2 * NJ)
        first = True
        first_dma = True
        pending_from = {}
        order = [2, 3, 4, 5, 6, 7, 0, 1]
        prev_upd = attn_done
        for s in order:
            t0 = 256 * s
            dps = list(prev_upd)
            if s == 0:
                dps.append(exch)
            grp = [(lambda c, k=k, s=s: s_view(c, 2 * s + k), dps) for k in range(2)]
            applies = []

            def dstf(k, ein, mu_ap, r_ap, s=s, applies=applies):
                last = None
                for c in range(8):
                    sv = s_view(c, 2 * s + k)
                    ee1 = tt("ve", tmps[:, c, 0:128], sv, mu_ap, ALU.subtract, [ein])
                    last = stt("ve", s2n[:, c, 128 * k:128 * k + 128],
                               tmps[:, c, 0:128], lngs[:, c:c + 1], r_ap,
                               ALU.mult, ALU.mult, [ee1, elg])
                    applies.append(last)
                return last
            ln_grp(grp, dstf)
            s2n_ev = applies[-1]
            if DBG["stop"] == "l1mA" and lpos == 0:
                e1 = dma("gp", dbg2_d[:, :, 0:256], s2n[:], applies + p.barrier())
                E("gp", lambda e: e.memset(onescol[0:1, 0:1], 1.0), [e1])
                return None
            sprev = (s - 1) % 8
            if sprev in pending_from:
                pe, col = pending_from.pop(sprev)
                u_evs.append(cp("ve", U[:, :, col:col + 1],
                                mpend[:, :, sprev:sprev + 1],
                                [pe] + applies[-2:]))
            w2acc = [None] * 8
            pg_guard = []
            buf_guard = mlp_state["g"]
            for j in range(NJ):
                bsel = j % 2
                d0 = attn_done if first_dma else []
                first_dma = False
                e_w1 = dma("sy", w1b[:, bsel].rearrange("p c n -> p (c n)"),
                           AP(w1p_d, l * 1024 * 2 * HPAD + 256 * j,
                              [[2 * HPAD, 128], [128 * 2 * HPAD, 8], [1, 256]]),
                           ([mlp_state["w1ld"]] if mlp_state["w1ld"] else []) +
                           buf_guard[bsel] + d0)
                mlp_state["w1ld"] = e_w1
                e_w2 = dma("sy", w2b[:, bsel].rearrange("p c n -> p (c n)"),
                           AP(w2p_d, l * HPAD * 1024 + 128 * j * 1024,
                              [[1024, 128], [128, 8], [1, 128]]),
                           ([mlp_state["w2ld"]] if mlp_state["w2ld"] else []) +
                           buf_guard[bsel] + d0)
                mlp_state["w2ld"] = e_w2
                ae = ge = None
                for c in range(8):
                    gg = p.barrier(("ve", "sc")) if (c == 0 and j == 0) \
                        else (pg_guard if c == 0 else [])
                    ae = mm(P0[:, 0:256], w1b[:, bsel, c, 0:128], s2n[:, c, :],
                            c == 0, False, [e_w1, s2n_ev] + gg)
                    ge = mm(P0[:, 256:512], w1b[:, bsel, c, 128:256], s2n[:, c, :],
                            False, c == 7, [e_w1, s2n_ev])
                if first:
                    ca = cb = None
                    for c in range(8):
                        ca = mm(P3[:, 0:1], w1b[:, bsel, c, 0:128],
                                lnbcol[:, c:c + 1], c == 0, False,
                                [e_w1, elbc] + (c2_ev[2 * j - 1:2 * j] if j else []))
                        cb = mm(P3[:, 1:2], w1b[:, bsel, c, 128:256],
                                lnbcol[:, c:c + 1], False, c == 7, [e_w1, elbc])
                    c2_ev[2 * j] = tt("ve", c2s[:, 2 * j:2 * j + 1], P3[:, 0:1],
                                      b1s[:, 2 * j:2 * j + 1], ALU.add, [ca, cb, eb1])
                    c2_ev[2 * j + 1] = tt("ve", c2s[:, 2 * j + 1:2 * j + 2],
                                          P3[:, 1:2], b1s[:, 2 * j + 1:2 * j + 2],
                                          ALU.add, [cb, eb1])
                hb = j % 2
                eg = E("sc", lambda e, o=gel[:, hb, :], i=P0[:, 256:512],
                       b=c2s[:, 2 * j + 1:2 * j + 2]:
                       e.activation(out=o, in_=i, func=AF.Gelu, bias=b),
                       [ge, c2_ev[2 * j + 1]] + buf_guard[hb])
                eh = stt("ve", h2[:, hb, :], P0[:, 0:256], c2s[:, 2 * j:2 * j + 1],
                         gel[:, hb, :], ALU.add, ALU.mult,
                         [ae, eg, c2_ev[2 * j]] + buf_guard[hb])
                pg_guard = [eh, eg]
                if DBG["stop"] == "l1mB" and lpos == 0 and j == 1:
                    e1 = dma("gp", dbg2_d[:, 0:1, 0:256], h2[:, hb:hb + 1, :],
                             [eh] + p.barrier())
                    E("gp", lambda e: e.memset(onescol[0:1, 0:1], 1.0), [e1])
                    return None
                for dc in range(8):
                    pps = (P1, P2)[dc // 4][:, 256 * (dc % 4):256 * (dc % 4) + 256]
                    gg = []
                    if j == 0 and dc % 2 == 0:
                        gg = p.barrier(("ve", "sc"))
                    w2acc[dc] = mm(pps, w2b[:, bsel, dc, :], h2[:, hb, :],
                                   j == 0 and dc % 2 == 0,
                                   j == NJ - 1 and dc % 2 == 1,
                                   [e_w2, eh] + gg)
                buf_guard[bsel] = [w2acc[7], ae, ge, eh]
            first = False
            if DBG["stop"] == "l1mC" and lpos == 0:
                e1 = dma("gp", dbg2_d[:, 0:1, 0:256], h2[:, 0:1, :],
                         [w2acc[7]] + p.barrier())
                E("gp", lambda e: e.memset(onescol[0:1, 0:1], 1.0), [e1])
                return None
            upd = []
            for dc in range(8):
                pps = (P1, P2)[dc // 4][:, 256 * (dc % 4):256 * (dc % 4) + 256]
                sl = dc % 2
                slot_g = upd[-3:-2] if dc >= 2 else [pg_guard[0]]
                ecp = cp("sc", gel[:, sl, 0:256], pps,
                         [w2acc[dc | 1], eb2] + slot_g)
                ee = stt("ve", U[:, dc, 1 + t0:1 + t0 + 255],
                         gel[:, sl, 0:255], b2s[:, dc:dc + 1],
                         U[:, dc, 1 + t0:1 + t0 + 255], ALU.add, ALU.add,
                         [ecp] + (upd[-1:] if upd else []))
                ep = stt("ve", mpend[:, dc, s:s + 1], gel[:, sl, 255:256],
                         b2s[:, dc:dc + 1], U[:, dc, t0 + 256:t0 + 257],
                         ALU.add, ALU.add, [ecp, ee])
                upd += [ee, ep]
            pending_from[s] = (upd[-1], t0 + 256)
            mlp_out += upd
            u_evs += upd
            prev_upd = upd[-2:]
            if DBG["stop"] == "l1s" + str(s) and lpos == 0:
                e1 = dma("gp", dbg2_d, U[:, :, 513:513 + HALO],
                         upd + p.barrier())
                E("gp", lambda e: e.memset(onescol[0:1, 0:1], 1.0), [e1])
                return None
        for s, (pe, col) in list(pending_from.items()):
            mlp_out.append(cp("ve", U[:, :, col:col + 1], mpend[:, :, s:s + 1],
                              [pe]))
            pending_from.pop(s)
        return mlp_out


    ev_wq = None
    for c in range(8):
        ev_wq = dma("sy", wqkv[:, c, :], wqkv_d[0, 128 * c:128 * c + 128, :],
                    [ev_wq] if ev_wq else [])

    ev_xh = dma("sy", Uh[:], xin[:, :, 0:HALO])
    if DBG["stop"] and DBG["stop"].startswith("u0g"):
        ngrp = int(DBG["stop"][3:])
        prev_apply = []
        for g in range(ngrp):
            t0 = 512 * g
            eld = dma("sy", xslab_u[:], xin[:, :, HALO + t0:HALO + t0 + 512],
                      prev_apply)
            grp = [(lambda c, k=k: xslab_u[:, c, 128 * k:128 * k + 128], [eld])
                   for k in range(4)]

            def dstf(k, ein, mu_ap, r_ap, t0=t0):
                last = None
                for c in range(8):
                    sv = xslab_u[:, c, 128 * k:128 * k + 128]
                    ee1 = tt("ve", tmps[:, c, 0:128], sv, mu_ap, ALU.subtract, [ein])
                    last = tt("ve", U[:, c, 1 + t0 + 128 * k:1 + t0 + 128 * k + 128],
                              tmps[:, c, 0:128], r_ap, ALU.mult, [ee1])
                return last
            prev_apply = ln_grp(grp, dstf)
        e1 = dma("gp", dbg_d, U[:], prev_apply)
        E("gp", lambda e: e.memset(onescol[0:1, 0:1], 1.0), [e1])
        return ctx
    if DBG["stop"] == "pre":
        e1 = dma("gp", dbg_d, U[:], [ev_xh, ev_c, ev_lamn, ev_lnw, ev_lnb])
        E("gp", lambda e: e.memset(onescol[0:1, 0:1], 1.0), [e1])
        return ctx
    u0 = u0_pass(xin, HALO, [ev_xh], [])
    if DBG["stop"] == "u0":
        e1 = dma("gp", dbg_d, U[:], u0)
        e2 = dma("gp", dbg2_d, Uh[:], u0)
        E("gp", lambda e: e.memset(onescol[0:1, 0:1], 1.0), [e1, e2])
        return ctx

    l1 = ffn_layer(0, 0, ev_wq, u0, u0)
    if l1 is None:
        return ctx
    if DBG["stop"] == "l1":
        e1 = dma("gp", dbg_d, U[:], l1 + p.barrier())
        E("gp", lambda e: e.memset(onescol[0:1, 0:1], 1.0), [e1])
        return ctx

    bar = p.barrier()
    ev_wq2 = None
    for c in range(8):
        ev_wq2 = dma("sy", wqkv[:, c, :], wqkv_d[1, 128 * c:128 * c + 128, :],
                     ([ev_wq2] if ev_wq2 else []) + (bar if c == 0 else []))
    exch1 = exchange_big(0, U[:, :, 1409:2049], l1[-1:] + bar, bar)
    l2 = ffn_layer(1, 1, ev_wq2, l1, [exch1])
    if DBG["stop"] == "ffn1":
        e1 = dma("gp", dbg_d, U[:], l2 + p.barrier())
        E("gp", lambda e: e.memset(onescol[0:1, 0:1], 1.0), [e1])
        return ctx

    bar = p.barrier()
    xupd = {}
    prev_ee = []
    for sg in range(4):
        t0 = 512 * sg
        eld = dma("sy", xslab_x[:], xin[:, :, HALO + t0:HALO + t0 + 512],
                  bar + prev_ee)
        ee = []
        for c in range(8):
            ee.append(stt("ve", xq_x[:, c, :], U[:, c, 1 + t0:1 + t0 + 512],
                          cc_("half"), xslab_x[:, c, :], ALU.mult, ALU.add,
                          [eld, ev_c] + bar + ([xupd[sg - 1]] if sg else [])))
        xupd[sg] = dma("gp", xpost_d[:, :, t0:t0 + 512], xq_x[:], ee)
        prev_ee = ee[-1:]
    if DBG["stop"] == "xupd":
        e1 = dma("gp", dbg_d[:, :, 0:T_LOC], xpost_d, list(xupd.values()))
        E("gp", lambda e: e.memset(onescol[0:1, 0:1], 1.0), [e1])
        return ctx

    bar_i = p.barrier()
    ev_itb = []
    for ti, nm in enumerate(("iAq", "iBq", "iA", "iB")):
        e0 = dma("gp", itb[:, ti], AP(tabs_d[nm], 0,
                                      [[64, 128], [128 * 64, 16], [1, 64]]),
                 bar_i + list(xupd.values()))
        ev_itb.append(dma_acc("gp", itb[:, ti],
                              AP(tabs_d[nm], 16 * 128 * 64,
                                 [[64, 128], [128 * 64, 16], [1, 64]]), [e0]))

    attn_x = {}
    prev_x2 = []
    prev_xw = None
    for so, sg in enumerate([2, 3, 0, 1]):
        t0 = 512 * sg
        bar0 = p.barrier(("ve", "sc")) + ([prev_xw] if prev_xw else [])
        eld = dma("sy", xslab_a[:], xpost_d[:, :, t0:t0 + 512],
                  [xupd[sg], prev_xw] + prev_x2)
        happ = []
        grp = [(lambda c, k=k: xslab_a[:, c, 128 * k:128 * k + 128],
                [eld] + bar0) for k in range(4)]

        def dsth(k, ein, mu_ap, r_ap, happ=happ):
            last = None
            for c in range(8):
                sv = xslab_a[:, c, 128 * k:128 * k + 128]
                ee1 = tt("ve", tmps[:, c, 0:128], sv, mu_ap, ALU.subtract, [ein])
                last = tt("ve", xq_a[:, c, 128 * k:128 * k + 128],
                          tmps[:, c, 0:128], r_ap, ALU.mult, [ee1])
                happ.append(last)
            return last
        ln_grp(grp, dsth)
        h_ev = happ[-1]

        rot_evs = []
        v_evs = []
        bar1 = p.barrier(("ve", "sc"))
        awb_g = aW_state["g"]
        for wi, dst_t in enumerate((aq, ak, av)):
            for hf in range(2):
                bsel = (wi * 2 + hf) % 2
                eW = dma("sy", aWb[:, bsel].rearrange("p c n -> p (c n)"),
                         AP(aW_d, wi * 1024 * 1024 + 512 * hf,
                            [[1024, 128], [128 * 1024, 8], [1, 512]]),
                         ([aW_state["ld"]] if aW_state["ld"] else []) +
                         awb_g[bsel] +
                         ([prev_xw] if (prev_xw and wi == 0 and hf == 0) else []))
                aW_state["ld"] = eW
                for k in range(4):
                    evm = None
                    for c in range(8):
                        gg = bar1 if (wi == 0 and hf == 0 and k == 0 and c == 0) \
                            else (p.barrier(("ve", "sc")) if c == 0 else [])
                        evm = mm(P3[:, 0:512], xq_a[:, c, 128 * k:128 * k + 128],
                                 aWb[:, bsel, c, :], c == 0, c == 7,
                                 [eW, h_ev] + gg)
                    awb_g[bsel] = [evm]
                    dcols = dst_t[:, k, 512 * hf:512 * hf + 512]
                    if wi == 2:
                        v_evs.append(cp("sc", dcols, P3[:, 0:512], [evm]))
                    else:
                        tA = itb[:, 0 if wi == 0 else 2, 4 * sg + k, :]
                        tB = itb[:, 1 if wi == 0 else 3, 4 * sg + k, :]
                        x3 = P3[:, 0:512].rearrange("p (h d) -> p h d", h=8)
                        Ab = AP(tA, 0, [tA.ap[0], [0, 8], [1, 64]])
                        perm = AP(x3, 1, [x3.ap[0], x3.ap[1], [2, 32], [-1, 2]])
                        Bb = AP(tB, 0, [tB.ap[0], [0, 8], [2, 32], [1, 2]])
                        e1 = tt("ve", awk[:, 0, :].rearrange(
                            "p (h d) -> p h d", h=8), x3, Ab, ALU.mult,
                            [evm] + ev_itb + rot_evs[-1:])
                        e2 = tt("ve", awk[:, 1, :].rearrange(
                            "p (h s d) -> p h s d", h=8, s=32), perm, Bb, ALU.mult,
                            [evm] + ev_itb)
                        rot_evs.append(tt("ve", dcols, awk[:, 0, :],
                                          awk[:, 1, :], ALU.add, [e1, e2]))
        qkv_ev = rot_evs + v_evs

        aT_evs = []
        ascr_g = []
        for k in range(4):
            q3 = aq[:, k, :]
            k3 = ak[:, k, :]
            sc_ev = {}
            for gi, off in ((0, 0), (1, 64)):
                qa = AP(q3, off, [q3.ap[0], [0, 8], [128, 8], [1, 64]])
                ka = AP(k3, off, [k3.ap[0], [128, 8], [0, 8], [1, 64]])
                e1 = tt("ve", ascr[:].rearrange("p (m n d) -> p m n d", m=8, n=8),
                        qa, ka, ALU.mult, qkv_ev + ascr_g)
                e2 = red("ve", (aS1, aS2)[gi][:].rearrange("p (m n) -> p m n", m=8),
                         ascr[:].rearrange("p (m n d) -> p m n d", m=8, n=8),
                         ALU.add, [e1])
                sc_ev[gi] = e2
                ascr_g = [e2]
            aw_evs = []
            for gi, (S, A) in enumerate(((aS1, aw1), (aS2, aw2))):
                mi = AP(S[:], 0, [S[:].ap[0], [1, 8], [8, 8]])
                em = red("ve", awm[:], mi, ALU.max, [sc_ev[gi]])
                mb = AP(awm[:], 0, [awm[:].ap[0], [0, 8], [1, 8]])
                es = tt("ve", A[:].rearrange("p (m n) -> p m n", m=8),
                        S[:].rearrange("p (m n) -> p m n", m=8), mb,
                        ALU.subtract, [em])
                ee = act(A[:], A[:], AF.Exp, [es])
                e2 = red("ve", awsum[:], AP(A[:], 0, [A[:].ap[0], [1, 8], [8, 8]]),
                         ALU.add, [ee])
                e3 = recip("ve", awsum[:], awsum[:], [e2])
                sb_ = AP(awsum[:], 0, [awsum[:].ap[0], [0, 8], [1, 8]])
                aw_evs.append(tt("ve", A[:].rearrange("p (m n) -> p m n", m=8),
                                 A[:].rearrange("p (m n) -> p m n", m=8), sb_,
                                 ALU.mult, [e3, ee]))
            eawc = stt("ve", aw2[:], aw2[:], lamneg[:, 0:1], aw1[:], ALU.mult,
                       ALU.add, [ev_lamn] + aw_evs)
            v3 = av[:, k, :]
            av_evs = []
            for n in range(8):
                va = AP(v3, 0, [v3.ap[0], [1, 128], [128, 8]])
                ab = AP(aw2[:], n, [aw2[:].ap[0], [0, 128], [8, 8]])
                e1 = tt("ve", ascr[:, 0:1024].rearrange("p (e m) -> p e m", e=128),
                        va, ab, ALU.mult, [eawc] + v_evs + ascr_g)
                rr = red("ve", aout[:, 128 * n:128 * n + 128],
                         ascr[:, 0:1024].rearrange("p (e m) -> p e m", e=128),
                         ALU.add, [e1])
                av_evs.append(rr)
                ascr_g = [rr]
            esq = act(ascr[:, 0:1024], aout[:], AF.Square, av_evs + ascr_g)
            er = red("ve", ss8[:], ascr[:, 0:1024].rearrange("p (n e) -> p n e", n=8),
                     ALU.add, [esq])
            ascr_g = [er]
            er = tss("ve", ss8[:], ss8[:], cc_("inv128"), ALU.mult, [er, ev_c])
            er = tss("ve", ss8[:], ss8[:], cc_("rms_eps"), ALU.add, [er])
            er = act(ss8[:], ss8[:], AF.Sqrt, [er])
            er = recip("ve", ss8[:], ss8[:], [er])
            apv = []
            for n in range(8):
                apv.append(tss("ve", aout[:, 128 * n:128 * n + 128],
                               aout[:, 128 * n:128 * n + 128], ss8[:, n:n + 1],
                               ALU.mult, [er]))
            lwb = AP(lnw_rep[:], 0, [lnw_rep[:].ap[0], [0, 8], [1, 128]])
            lbb = AP(lnb_rep[:], 0, [lnb_rep[:].ap[0], [0, 8], [1, 128]])
            ea1 = tt("ve", aout[:].rearrange("p (n e) -> p n e", n=8),
                     aout[:].rearrange("p (n e) -> p n e", n=8), lwb, ALU.mult,
                     apv + [ev_lnw])
            ea2 = tt("ve", aout[:].rearrange("p (n e) -> p n e", n=8),
                     aout[:].rearrange("p (n e) -> p n e", n=8), lbb, ALU.add,
                     [ea1, ev_lnb])
            for ac in range(8):
                et = tr(P0[:, 0:128], aout[:, 128 * ac:128 * ac + 128],
                        [ea2] + (aT_evs[-1:] if aT_evs else []))
                aT_evs.append(cp("sc", aTs[:, ac, 128 * k:128 * k + 128],
                                 P0[:, 0:128], [et]))

        x2 = []
        bank_ev = {}
        awo_g = awo_state["g"]
        for dc in range(8):
            pps = (P1, P2)[(dc // 2) % 2][:, 512 * (dc % 2):512 * (dc % 2) + 512]
            evm = None
            for ac in range(8):
                buf = aWob[:, ac % 2, :]
                ld = dma("sy", buf, aW_d[3, 128 * ac:128 * ac + 128,
                                         128 * dc:128 * dc + 128],
                         ([awo_state["ld"]] if awo_state["ld"] else []) +
                         awo_g[ac % 2])
                awo_state["ld"] = ld
                dd = [ld] + aT_evs
                if ac == 0 and dc - 4 in bank_ev:
                    dd.append(bank_ev[dc - 4])
                if ac == 0 and dc == 0:
                    dd += p.barrier(("ve", "sc"))
                evm = mm(pps, buf, aTs[:, ac, :], ac == 0, ac == 7, dd)
                awo_g[ac % 2] = [evm]
            e = tt("ve", xslab_a[:, dc, :], xslab_a[:, dc, :], pps, ALU.add,
                   [evm, eld])
            bank_ev[dc] = e
            x2.append(e)
        attn_x[sg] = dma("gp", xpost_d[:, :, t0:t0 + 512], xslab_a[:], x2)
        prev_x2 = x2[-1:]
        prev_xw = attn_x[sg]
        if sg == 3:
            eread = dma("gp", xch_in[1], xpost_d[:, :, T_LOC - HALO:T_LOC],
                        [attn_x[2], attn_x[3]])
            ecc2 = E("gp", lambda e: e.collective_compute(
                "AllGather", ALU.bypass,
                replica_groups=[[0, 1], [2, 3], [4, 5], [6, 7]],
                ins=[xch_in[1][:]], outs=[xch_out[1][:]]),
                [eread], sem="cc", inc=1)

    bar = p.barrier()
    e3 = dma("gp", Uh[:], xch_out[1][0], [ecc2] + bar)
    e4 = dma("gp", halo_a[:], xch_out[1][1], [ecc2, attn_x[0], attn_x[1]])
    e5 = tss("ve", Uh[:], Uh[:], flags[:, 0:1], ALU.mult, [e3, ev_fl] + bar)
    exch2 = stt("ve", Uh[:], halo_a[:], flags[:, 1:2], Uh[:], ALU.mult, ALU.add,
                [e4, e5])

    if DBG["stop"] == "attn":
        e1 = dma("gp", dbg_d[:, :, 0:T_LOC], xpost_d, list(attn_x.values()))
        E("gp", lambda e: e.memset(onescol[0:1, 0:1], 1.0), [e1])
        return ctx

    ev_wq = None
    for c in range(8):
        ev_wq = dma("sy", wqkv[:, c, :], wqkv_d[0, 128 * c:128 * c + 128, :],
                    ([ev_wq] if ev_wq else []) + (bar if c == 0 else []))
    u0b = u0_pass(xpost_d, 0, [exch2], list(attn_x.values()) + bar + [exch2])
    if DBG["stop"] == "u0b":
        e1 = dma("gp", dbg_d, U[:], u0b)
        e2 = dma("gp", dbg2_d, Uh[:], u0b)
        E("gp", lambda e: e.memset(onescol[0:1, 0:1], 1.0), [e1, e2])
        return ctx
    l1b = ffn_layer(0, 2, ev_wq, u0b, u0b)
    bar = p.barrier()
    ev_wq2 = None
    for c in range(8):
        ev_wq2 = dma("sy", wqkv[:, c, :], wqkv_d[1, 128 * c:128 * c + 128, :],
                     ([ev_wq2] if ev_wq2 else []) + (bar if c == 0 else []))
    exch3 = exchange_big(2, U[:, :, 1409:2049], l1b[-1:] + bar, bar)
    l2b = ffn_layer(1, 3, ev_wq2, l1b, [exch3])

    bar = p.barrier()
    outs = []
    prev_ee = []
    for sg in range(4):
        t0 = 512 * sg
        eld = dma("sy", xslab_x[:], xpost_d[:, :, t0:t0 + 512], bar + prev_ee)
        ee = []
        for c in range(8):
            ee.append(stt("ve", xq_x[:, c, :], U[:, c, 1 + t0:1 + t0 + 512],
                          cc_("half"), xslab_x[:, c, :], ALU.mult, ALU.add,
                          [eld, ev_c] + bar + ([outs[-1]] if outs else [])))
        outs.append(dma("gp", out_d[:, :, t0:t0 + 512], xq_x[:], ee))
        prev_ee = ee[-1:]
    E("gp", lambda e: e.memset(onescol[0:1, 0:1], 1.0), outs)
    return ctx



def _build_emit(nc):
    p = Prog()
    ctx = build(nc, p)
    p.finalize()
    sem_ctx, sems = [], {}
    for s in p.sem_names():
        h = nc.semaphore(s)
        sem_ctx.append(h)
        sems[s] = h.__enter__()
    return p, ctx, sem_ctx, sems


def _run(in_maps):
    nc = bass.Bass("TRN2", target_bir_lowering=False)
    p, ctx, sem_ctx, sems = _build_emit(nc)
    with nc.Block() as block:
        def mk(ename):
            def body(eng):
                for st in p.steps[ename]:
                    for (s, v) in st.waits:
                        eng.wait_ge(sems[s], v)
                    inst = st.fn(eng)
                    if st.val is not None:
                        inst.then_inc(sems[st.sem], st.inc)
            return body
        block.tensor(mk("te"))
        block.vector(mk("ve"))
        block.scalar(mk("sc"))
        block.sync(mk("sy"))
        block.gpsimd(mk("gp"))
    for t in reversed(ctx):
        t.__exit__(None, None, None)
    for h in reversed(sem_ctx):
        h.__exit__(None, None, None)
    return run_bass_kernel_spmd(nc, in_maps, core_ids=list(range(8)))


def kernel(**inputs):
    x = np.asarray(inputs["L0_attn_x"])
    W = _prep_shared({k: np.asarray(v) for k, v in inputs.items()})
    in_maps = [_make_core_inputs(c, x, W) for c in range(8)]
    res = _run(in_maps)
    out = np.zeros((4, 4096, D), np.float32)
    for c in range(8):
        seq, half = c // 2, c % 2
        o = res.results[c]["out"]
        out[seq, half * T_LOC:(half + 1) * T_LOC] = (
            o.transpose(2, 1, 0).reshape(T_LOC, D))
    return out


# version 04 (final)
# speedup vs baseline: 1.0341x; 1.0341x over previous
import math
import numpy as np

import concourse.bass as bass
import concourse.mybir as mybir
from concourse.bass_utils import run_bass_kernel_spmd

F32 = mybir.dt.float32
AF = mybir.ActivationFunctionType
ALU = mybir.AluOpType

T_LOC = 2048
HALO = 640
D = 1024
HID = 2730
NJ = 22
HPAD = NJ * 128
LI = 0.8 - 0.6 * math.exp(-0.3 * 12)

DBG = {"stop": None}



def _hilo(a64):
    hi = a64.astype(np.float32)
    lo = (a64 - hi.astype(np.float64)).astype(np.float32)
    return np.stack([hi, lo], 0)


def _xpos_tables(g0):
    pos = np.arange(g0 - HALO, g0 + T_LOC, dtype=np.float64)
    half = np.arange(0, 64, 2, dtype=np.float64)
    inv_freq = 1.0 / 10000.0 ** (half / 64.0)
    f = pos[:, None] * inv_freq[None, :]
    f = np.concatenate([f, f], 1)
    base = (half + 0.4 * 64.0) / (1.4 * 64.0)
    power = (pos - 4096 // 2) / float(512 // 2)
    sc = base[None, :] ** power[:, None]
    sc = np.concatenate([sc, sc], 1)
    sgn = np.where(np.arange(64) < 32, -1.0, 1.0)[None, :]
    cos, sin = np.cos(f), np.sin(f)
    Aq = cos * sc * 8.0
    Bq = sgn * sin * sc * 8.0
    Ak = cos / sc
    Bk = sgn * sin / sc
    return [_hilo(t).reshape(2, 21, 128, 64) for t in (Aq, Bq, Ak, Bk)]


def _attn_tables(g0):
    pos = np.arange(g0, g0 + T_LOC, dtype=np.float64)
    inv_freq = 1.0 / 10000.0 ** (np.arange(0, 64, 2, dtype=np.float64) / 64.0)
    f = np.repeat(pos[:, None] * inv_freq[None, :], 2, axis=1)
    sgn = np.where(np.arange(64) % 2 == 0, -1.0, 1.0)[None, :]
    cos, sin = np.cos(f), np.sin(f)
    s4 = 64.0 ** (-0.25)
    return [_hilo(t).reshape(2, 16, 128, 64)
            for t in (cos * s4, sgn * sin * s4, cos, sgn * sin)]


def _fm(x):
    T = x.shape[0]
    return np.ascontiguousarray(x.reshape(T, 8, 128).transpose(2, 1, 0))


CONST_NAMES = ["ln_eps", "rms_eps", "li", "one_m_li", "inv_d",
               "inv128", "half", "one"]
CONST_VALS = [1e-5, 1e-8, LI, 1.0 - LI, 1.0 / 1024.0, 1.0 / 128.0, 0.5, 1.0]


def _prep_shared(inp):
    W = {}
    W["wqkv"] = np.ascontiguousarray(inp["L0_ffn_wqkv"].astype(np.float32))
    W["wo_w"] = np.ascontiguousarray(inp["L0_ffn_wo"].astype(np.float32))
    w1 = inp["L0_ffn_w1"].astype(np.float32)
    b1 = inp["L0_ffn_b1"].astype(np.float32)
    w1p = np.zeros((2, 1024, 2 * HPAD), np.float32)
    b1p = np.zeros((2, 2 * HPAD), np.float32)
    for j in range(NJ):
        a0 = 128 * j
        n = min(128, HID - a0)
        if n <= 0:
            continue
        w1p[:, :, 256 * j:256 * j + n] = w1[:, :, a0:a0 + n]
        w1p[:, :, 256 * j + 128:256 * j + 128 + n] = w1[:, :, HID + a0:HID + a0 + n]
        b1p[:, 256 * j:256 * j + n] = b1[:, a0:a0 + n]
        b1p[:, 256 * j + 128:256 * j + 128 + n] = b1[:, HID + a0:HID + a0 + n]
    W["w1p"] = w1p
    W["b1p"] = np.ascontiguousarray(b1p.reshape(2, 2 * NJ, 128).transpose(0, 2, 1))
    w2p = np.zeros((2, HPAD, 1024), np.float32)
    w2p[:, :HID] = inp["L0_ffn_w2"].astype(np.float32)
    W["w2p"] = w2p
    W["b2v"] = np.ascontiguousarray(
        inp["L0_ffn_b2"].astype(np.float32).reshape(2, 8, 128).transpose(0, 2, 1))
    W["lngv"] = np.ascontiguousarray(
        inp["L0_ffn_ln_g"].astype(np.float32).reshape(2, 8, 128).transpose(0, 2, 1))
    W["lnbv"] = np.ascontiguousarray(
        inp["L0_ffn_ln_b"].astype(np.float32).reshape(2, 8, 128).transpose(0, 2, 1))
    W["lnbc"] = np.ascontiguousarray(
        inp["L0_ffn_ln_b"].astype(np.float32).reshape(2, 8, 128).transpose(0, 2, 1))
    W["qsv"] = np.ascontiguousarray(inp["L0_ffn_q_scale"].astype(np.float32))
    W["ksv"] = np.ascontiguousarray(inp["L0_ffn_k_scale"].astype(np.float32))
    W["aW"] = np.stack([inp["L0_attn_Wq"], inp["L0_attn_Wk"],
                        inp["L0_attn_Wv"], inp["L0_attn_Wo"]]).astype(np.float32)
    W["alnw"] = inp["L0_attn_ln_w"].astype(np.float32).reshape(1, 128)
    W["alnb"] = inp["L0_attn_ln_b"].astype(np.float32).reshape(1, 128)
    W["lamv"] = np.ascontiguousarray(np.stack(
        [inp["L0_attn_lambda_q1"], inp["L0_attn_lambda_k1"],
         inp["L0_attn_lambda_q2"], inp["L0_attn_lambda_k2"]]).astype(np.float32))
    return W


def _make_core_inputs(c, x, W):
    seq, half = c // 2, c % 2
    g0 = half * T_LOC
    d = dict(W)
    xs = np.zeros((HALO + T_LOC, D), np.float32)
    lo = g0 - HALO
    src_lo = max(lo, 0)
    xs[src_lo - lo:] = x[seq, src_lo:g0 + T_LOC]
    d["xin"] = _fm(xs)
    xt = _xpos_tables(g0)
    d["xAq"], d["xBq"], d["xAk"], d["xBk"] = xt
    it = _attn_tables(g0)
    d["iAq"], d["iBq"], d["iA"], d["iB"] = it
    iu = np.arange(128)
    utri = np.where(iu[None, :] >= iu[:, None], 0.0, -30000.0).astype(np.float32)
    ltri = np.where(iu[None, :] <= iu[:, None], 0.0, -30000.0).astype(np.float32)
    d["masks"] = np.ascontiguousarray(np.stack([utri, ltri]))
    fl = np.zeros((128, 18), np.float32)
    if half == 1:
        fl[:, 0] = 1.0
    for tau in range(16):
        if half == 0 and tau < 4:
            fl[:, 2 + tau] = -30000.0
    d["flags"] = fl
    cv = np.zeros((128, len(CONST_VALS) * 2), np.float32)
    for i, v in enumerate(CONST_VALS):
        h = np.float32(v)
        cv[:, 2 * i] = h
        cv[:, 2 * i + 1] = np.float32(np.float64(v) - np.float64(h))
    d["consts"] = cv
    d["identm"] = np.eye(128, dtype=np.float32)
    return d



ENGINES = ("te", "ve", "sc", "sy", "gp")
ENG_SEM = {"te": "te", "ve": "ve", "sc": "sc", "sy": "sy", "gp": "gp"}


class Step:
    __slots__ = ("fn", "deps", "sem", "inc", "eng", "need", "val", "waits")

    def __init__(self, fn, deps, sem, inc, eng):
        self.fn, self.deps, self.sem, self.inc, self.eng = fn, deps, sem, inc, eng
        self.need = False
        self.val = None
        self.waits = ()


DMA_SLOTS = {"dsy": 12, "dgp": 6}


class Prog:
    def __init__(self):
        self.steps = {e: [] for e in ENGINES}
        self.slot_rr = {"dsy": 0, "dgp": 0}
        self.slot_prev = {}

    def emit(self, eng, fn, deps=(), sem=None, inc=None):
        if sem is None:
            sem = ENG_SEM[eng]
        deps = [d for d in deps if d is not None]
        if inc is None:
            inc = 16 if sem in ("dsy", "dgp") else 1
        if sem in ("dsy", "dgp"):
            q = sem
            slot = self.slot_rr[q]
            self.slot_rr[q] = (slot + 1) % DMA_SLOTS[q]
            sem = f"{q}{slot}"
            prev = self.slot_prev.get(sem)
            if prev is not None:
                deps = deps + [prev]
        st = Step(fn, deps, sem, inc, eng)
        if sem.startswith("dsy") or sem.startswith("dgp"):
            self.slot_prev[sem] = st
        self.steps[eng].append(st)
        return st

    def last(self, eng):
        return self.steps[eng][-1] if self.steps[eng] else None

    def barrier(self, engs=("te", "ve", "sc")):
        return [self.last(e) for e in engs if self.last(e) is not None]

    def sem_names(self):
        s = set()
        for e in ENGINES:
            for st in self.steps[e]:
                s.add(st.sem)
        return sorted(s)

    def finalize(self):
        for e in ENGINES:
            for st in self.steps[e]:
                if st.sem.startswith("dsy") or st.sem.startswith("dgp"):
                    st.need = True
                for dd in st.deps:
                    dd.need = True
        cnt = {}
        for e in ENGINES:
            for st in self.steps[e]:
                if st.need:
                    cnt[st.sem] = cnt.get(st.sem, 0) + st.inc
                    st.val = cnt[st.sem]
        for e in ENGINES:
            hwm = {}
            for st in self.steps[e]:
                waits = []
                for dd in st.deps:
                    if dd.val is None:
                        continue
                    if hwm.get(dd.sem, 0) >= dd.val:
                        continue
                    waits.append((dd.sem, dd.val))
                    hwm[dd.sem] = dd.val
                st.waits = waits



def build(nc, p):
    dt = nc.dram_tensor
    xin = dt("xin", [128, 8, HALO + T_LOC], F32, kind="ExternalInput").ap()
    wqkv_d = dt("wqkv", [2, 1024, 1536], F32, kind="ExternalInput").ap()
    wo_d = dt("wo_w", [2, 512, 1024], F32, kind="ExternalInput").ap()
    w1p_d = dt("w1p", [2, 1024, 2 * HPAD], F32, kind="ExternalInput").ap()
    b1p_d = dt("b1p", [2, 128, 2 * NJ], F32, kind="ExternalInput").ap()
    w2p_d = dt("w2p", [2, HPAD, 1024], F32, kind="ExternalInput").ap()
    b2v_d = dt("b2v", [2, 128, 8], F32, kind="ExternalInput").ap()
    lng_d = dt("lngv", [2, 128, 8], F32, kind="ExternalInput").ap()
    lnb_d = dt("lnbv", [2, 128, 8], F32, kind="ExternalInput").ap()
    lnbc_d = dt("lnbc", [2, 128, 8], F32, kind="ExternalInput").ap()
    qsv_d = dt("qsv", [2, 64], F32, kind="ExternalInput").ap()
    ksv_d = dt("ksv", [2, 64], F32, kind="ExternalInput").ap()
    aW_d = dt("aW", [4, 1024, 1024], F32, kind="ExternalInput").ap()
    alnw_d = dt("alnw", [1, 128], F32, kind="ExternalInput").ap()
    alnb_d = dt("alnb", [1, 128], F32, kind="ExternalInput").ap()
    lam_d = dt("lamv", [4, 64], F32, kind="ExternalInput").ap()
    tabs_d = {}
    for nm in ("xAq", "xBq", "xAk", "xBk"):
        tabs_d[nm] = dt(nm, [2, 21, 128, 64], F32, kind="ExternalInput").ap()
    for nm in ("iAq", "iBq", "iA", "iB"):
        tabs_d[nm] = dt(nm, [2, 16, 128, 64], F32, kind="ExternalInput").ap()
    masks_d = dt("masks", [2, 128, 128], F32, kind="ExternalInput").ap()
    flags_d = dt("flags", [128, 18], F32, kind="ExternalInput").ap()
    consts_d = dt("consts", [128, len(CONST_VALS) * 2], F32, kind="ExternalInput").ap()
    ident_d = dt("identm", [128, 128], F32, kind="ExternalInput").ap()
    out_d = dt("out", [128, 8, T_LOC], F32, kind="ExternalOutput").ap()

    xpost_d = dt("xpost", [128, 8, T_LOC], F32).ap()
    bounce_d = dt("bounce", [80, 2, 8, 128], F32).ap()
    xch_in = [dt(f"xci{i}", [128, 8, HALO], F32).ap() for i in range(3)]
    xch_out = [dt(f"xco{i}", [2, 128, 8, HALO], F32).ap() for i in range(3)]
    sch_in = [dt(f"sci{i}", [128, 8, 1], F32).ap() for i in range(4)]
    sch_out = [dt(f"sco{i}", [2, 128, 8, 1], F32).ap() for i in range(4)]
    dbg_d = dbg2_d = None
    if DBG["stop"]:
        dbg_d = dt("dbg", [128, 8, 2049], F32, kind="ExternalOutput").ap()
        dbg2_d = dt("dbg2", [128, 8, HALO], F32, kind="ExternalOutput").ap()

    ctx = []

    def sb(name, shape):
        t = nc.sbuf_tensor(name, shape, F32)
        ctx.append(t)
        return t.__enter__()

    def psa(name, shape):
        t = nc.psum_tensor(name, shape, F32)
        ctx.append(t)
        return t.__enter__()

    AR_N = 49160
    AR = sb("AR", [128, AR_N])

    def V(off, shape):
        free = shape[1:]
        n = 1
        for s in free:
            n *= s
        ap = AR[:, off:off + n]
        if len(free) <= 1:
            return ap
        pat = "p (" + " ".join(f"d{i}" for i in range(len(free))) + ") -> p " + \
            " ".join(f"d{i}" for i in range(len(free)))
        return ap.rearrange(pat, **{f"d{i}": s for i, s in enumerate(free)})

    class VT:
        def __init__(self, off, shape):
            self.off = off
            self.shape = shape

        def __getitem__(self, idx):
            return V(self.off, self.shape)[idx]

    U = VT(0, [128, 8, 2049])
    WQKV_O, KTR_O, VR_O, QTW_O, UH_O = 16392, 28680, 32776, 36872, 38920
    SUB_O = 44040
    wqkv = VT(WQKV_O, [128, 8, 1536])
    kTr = VT(KTR_O, [128, 4, 8, 128])
    vr = VT(VR_O, [128, 8, 512])
    qTw = VT(QTW_O, [128, 4, 4, 128])
    Uh = VT(UH_O, [128, 8, HALO])
    xtf = VT(SUB_O, [128, 4, 4, 64])
    wk = VT(SUB_O + 1024, [128, 3, 512])
    Pt = VT(SUB_O + 2560, [128, 2, 640])
    PTs = VT(SUB_O + 3840, [128, 640])
    wob = VT(SUB_O + 4480, [128, 2, 128])
    qsq = VT(SUB_O + 1024, [128, 512])
    oTs = VT(UH_O, [128, 4, 512])
    osb = VT(UH_O + 2048, [128, 4, 512])
    w1b = VT(UH_O, [128, 2, 8, 256])
    w2b = VT(QTW_O, [128, 2, 8, 128])
    s2n = VT(SUB_O, [128, 8, 256])
    h2 = VT(SUB_O + 2048, [128, 2, 256])
    gel = VT(SUB_O + 2560, [128, 2, 256])
    halo_a = VT(SUB_O, [128, 8, HALO])
    xslab_u = VT(KTR_O, [128, 8, 512])
    xslab_x = VT(UH_O, [128, 8, 512])
    xq_x = VT(SUB_O, [128, 8, 512])
    aq = VT(0, [128, 4, 1024])
    ak = VT(4096, [128, 4, 1024])
    av = VT(8192, [128, 4, 1024])
    aWb = VT(12288, [128, 2, 8, 512])
    ascr = VT(20480, [128, 4096])
    aTs = VT(24576, [128, 8, 512])
    aout = VT(28672, [128, 1024])
    itb = VT(SUB_O, [128, 4, 16, 64])
    awk = VT(29696, [128, 2, 512])
    xslab_a = VT(34824, [128, 8, 512])
    xq_a = VT(24576, [128, 8, 512])

    ident = sb("ident", [128, 128])
    maskU = sb("maskU", [128, 128])
    maskL = sb("maskL", [128, 128])
    flags = sb("flagsS", [128, 18])
    consts = sb("constsS", [128, len(CONST_VALS)])
    consts_raw = sb("constsR", [128, len(CONST_VALS) * 2])
    onescol = sb("onescol", [128, 1])
    onesrow = sb("onesrow", [1, 128])
    lamneg = sb("lamneg", [128, 1])
    lnw_rep = sb("lnwrep", [128, 128])
    lnb_rep = sb("lnbrep", [128, 128])
    b1s = sb("b1s", [128, 2 * NJ])
    b2s = sb("b2s", [128, 8])
    lngs = sb("lngs", [128, 8])
    lnbcol = sb("lnbcol", [128, 8])
    qs_rep = sb("qsrep", [128, 64])
    qsp_rep = sb("qsprep", [128, 64])
    ks_rep = sb("ksrep", [128, 64])
    ksp_rep = sb("ksprep", [128, 64])
    c2s = sb("c2s", [128, 2 * NJ])
    lamt = sb("lamt", [1, 256])
    lamr = sb("lamr", [1, 8])
    rnq = sb("rnq", [128, 4, 8])
    rnqn = sb("rnqn", [128, 4, 8])
    rnk = sb("rnk", [128, 8])
    rcp = sb("rcp", [128, 8])
    pend = sb("pend", [128, 8, 1])
    mpend = sb("mpend", [128, 8, 8])
    mx = sb("mx", [128, 8])
    tmps = sb("tmps", [128, 8, 128])
    stat1 = sb("stat1", [128, 4])
    stat2 = sb("stat2", [128, 4])
    statmu = sb("statmu", [128, 4])
    statr = sb("statr", [128, 4])
    statpk = sb("statpk", [128, 8])
    statT = sb("statT", [8, 128])
    bcast = VT(VR_O, [128, 2, 4, 128])
    sqT = VT(VR_O + 1024, [128, 1024])
    aS1 = sb("aS1", [128, 64])
    aS2 = sb("aS2", [128, 64])
    aw1 = sb("aw1", [128, 64])
    aw2 = sb("aw2", [128, 64])
    awm = sb("awm", [128, 8])
    awsum = sb("awsum", [128, 8])
    ss8 = sb("ss8", [128, 8])
    aWob = sb("aWob", [128, 2, 128])
    sxa = sb("sxa", [128, 8, 1])
    sxb = sb("sxb", [128, 8, 1])
    sxc = sb("sxc", [128, 8, 1])

    P0 = psa("P0", [128, 1024])
    P1 = psa("P1", [128, 1024])
    P2 = psa("P2", [128, 1024])
    P3 = psa("P3", [128, 1024])

    E = p.emit

    def dma(q, out, in_, deps=()):
        return E(q, lambda e, o=out, i=in_: e.dma_start(out=o, in_=i), deps,
                 sem="dsy" if q == "sy" else "dgp")

    def mm(out, lhsT, rhs, start, stop, deps=()):
        return E("te", lambda e, o=out, l=lhsT, r=rhs, s0=start, s1=stop:
                 e.matmul(o, l, r, start=s0, stop=s1, skip_group_check=True), deps)

    def tr(out, in_, deps=()):
        return E("te", lambda e, o=out, i=in_: e.transpose(o, i, ident[:]), deps)

    def tt(eng, out, a, b, op, deps=()):
        return E(eng, lambda e, o=out, x=a, y=b, z=op: e.tensor_tensor(o, x, y, z), deps)

    def tss(eng, out, a, s1, op, deps=()):
        return E(eng, lambda e, o=out, x=a, s=s1, z=op:
                 e.tensor_single_scalar(o, x, s, z), deps)

    def stt(eng, out, in0, scalar, in1, op0, op1, deps=()):
        return E(eng, lambda e, o=out, x=in0, s=scalar, y=in1, za=op0, zb=op1:
                 e.scalar_tensor_tensor(o, x, s, y, za, zb), deps)

    def act(out, in_, func, deps=(), bias=0.0, scale=1.0):
        return E("sc", lambda e, o=out, i=in_, f=func, b=bias, s=scale:
                 e.activation(out=o, in_=i, func=f, bias=b, scale=s), deps)

    def red(eng, out, in_, op, deps=(), negate=False):
        return E(eng, lambda e, o=out, i=in_, z=op, n=negate:
                 e.tensor_reduce(o, i, axis=mybir.AxisListType.X, op=z, negate=n), deps)

    def recip(eng, out, in_, deps=()):
        return E(eng, lambda e, o=out, i=in_: e.reciprocal(o, i), deps)

    def cp(eng, out, in_, deps=()):
        if eng == "sc":
            return E(eng, lambda e, o=out, i=in_: e.copy(out=o, in_=i), deps)
        return E(eng, lambda e, o=out, i=in_: e.tensor_copy(o, i), deps)

    def mset(eng, out, v, deps=()):
        return E(eng, lambda e, o=out, vv=v: e.memset(o, vv), deps)

    def AP(base, off_extra, dims):
        return bass.AP(base.tensor, base.offset + off_extra, dims)

    def cc_(name):
        i = CONST_NAMES.index(name)
        return consts[:, i:i + 1]

    ev_id = dma("sy", ident[:], ident_d)
    ev_mU = dma("sy", maskU[:], masks_d[0])
    ev_mL = dma("sy", maskL[:], masks_d[1])
    ev_fl = dma("sy", flags[:], flags_d)
    ev_cr = dma("sy", consts_raw[:], consts_d)
    nc_ = len(CONST_VALS)
    ev_c = tt("ve", consts[:],
              AP(consts_raw[:], 0, [consts_raw[:].ap[0], [2, nc_]]),
              AP(consts_raw[:], 1, [consts_raw[:].ap[0], [2, nc_]]),
              ALU.add, [ev_cr])
    ev_ones = mset("ve", onescol[:], 1.0)
    ev_onesr = mset("ve", onesrow[:], 1.0)
    ev_stT = mset("ve", statT[:], 0.0)
    ev_lam = dma("sy", lamt[:], AP(lam_d, 0, [[256, 1], [1, 256]]))
    e1 = tt("ve", lamr[0:1, 0:1].rearrange("a b -> a b") if False else lamt[0:1, 0:64],
            lamt[0:1, 0:64], lamt[0:1, 64:128], ALU.mult, [ev_lam])
    e2 = tt("ve", lamt[0:1, 64:128], lamt[0:1, 128:192], lamt[0:1, 192:256],
            ALU.mult, [ev_lam, e1])
    e3 = red("ve", lamr[0:1, 0:1], lamt[0:1, 0:64], ALU.add, [e1, e2])
    e4 = red("ve", lamr[0:1, 1:2], lamt[0:1, 64:128], ALU.add, [e2])
    e5 = act(lamr[0:1, 2:4], lamr[0:1, 0:2], AF.Exp, [e3, e4])
    e6 = tt("ve", lamr[0:1, 4:5], lamr[0:1, 3:4], lamr[0:1, 2:3], ALU.subtract, [e5])
    e7 = tss("ve", lamr[0:1, 4:5], lamr[0:1, 4:5],
             consts[0:1, CONST_NAMES.index("li"):CONST_NAMES.index("li") + 1],
             ALU.subtract, [e6, ev_c])
    e8 = mm(P3[:, 0:1], onesrow[:], lamr[0:1, 4:5], True, True, [e7, ev_onesr])
    ev_lamn = cp("ve", lamneg[:], P3[:, 0:1], [e8])

    ev_lnw = dma("sy", lnw_rep[:], AP(alnw_d, 0, [[0, 128], [1, 128]]))
    ev_lnb = dma("sy", lnb_rep[:], AP(alnb_d, 0, [[0, 128], [1, 128]]))
    ev_lnw = tss("ve", lnw_rep[:], lnw_rep[:], cc_("one_m_li"), ALU.mult,
                 [ev_lnw, ev_c])
    ev_lnb = tss("ve", lnb_rep[:], lnb_rep[:], cc_("one_m_li"), ALU.mult,
                 [ev_lnb, ev_c])

    def dma_acc(q, out, in_, deps=()):
        return E(q, lambda e, o=out, i=in_:
                 e.dma_start(out=o, in_=i, accum_op=ALU.add), deps,
                 sem="dsy" if q == "sy" else "dgp")

    bounce_ctr = [0]

    def ln_grp(grp, dstf):
        ng = len(grp)
        bar_ls = p.barrier(("ve", "sc"))
        evs = []
        prevk = bar_ls
        for k, (srcf, dps) in enumerate(grp):
            evt = [tr(P3[:, 128 * c:128 * c + 128], srcf(c),
                      dps + [ev_id] + (prevk if c == 0 else []))
                   for c in range(8)]
            e1_ = red("ve", stat1[:, k:k + 1], P3[:], ALU.add, evt)
            esq = act(sqT[:], P3[:], AF.Square, evt + [e1_])
            e2_ = red("ve", stat2[:, k:k + 1], sqT[:], ALU.add, [esq])
            evs += [e1_, e2_, esq]
            prevk = [esq, e1_, e2_]
        em = tss("ve", statmu[:, 0:ng], stat1[:, 0:ng], cc_("inv_d"), ALU.mult,
                 evs + [ev_c])
        ev = tss("ve", statr[:, 0:ng], stat2[:, 0:ng], cc_("inv_d"), ALU.mult, evs)
        et = tt("ve", statpk[:, 0:ng], statmu[:, 0:ng], statmu[:, 0:ng], ALU.mult, [em])
        ev = tt("ve", statr[:, 0:ng], statr[:, 0:ng], statpk[:, 0:ng],
                ALU.subtract, [ev, et])
        ev = tss("ve", statr[:, 0:ng], statr[:, 0:ng], cc_("ln_eps"), ALU.add, [ev])
        es = act(statr[:, 0:ng], statr[:, 0:ng], AF.Sqrt, [ev])
        ev = recip("ve", statr[:, 0:ng], statr[:, 0:ng], [es])
        ea = cp("ve", statpk[:, 0:ng], statmu[:, 0:ng], [em, et])
        eb = cp("ve", statpk[:, ng:2 * ng], statr[:, 0:ng], [ev, ea])
        ett = tr(P2[0:2 * ng, 0:128], statpk[:, 0:2 * ng], [eb])
        ecc = cp("sc", statT[0:2 * ng, :], P2[0:2 * ng, 0:128], [ett, ev_stT])
        slot = bounce_ctr[0] % 80
        bounce_ctr[0] += 1
        bv = bounce_d[slot]
        eo1 = dma("gp", bv[0, 0:ng, :], statT[0:ng, :], [ecc])
        eo2 = dma("gp", bv[1, 0:ng, :], statT[ng:2 * ng, :], [ecc])
        ein = dma("sy", bcast[:, :, 0:ng, :],
                  AP(bv, 0, [[0, 128], [8 * 128, 2], [128, ng], [1, 128]]),
                  [eo1, eo2] + bar_ls)
        res = []
        for k in range(ng):
            res.append(dstf(k, ein, bcast[:, 0, k, :], bcast[:, 1, k, :]))
        return res

    def u0_pass(src_main_d, main_off, halo_deps, deps):
        all_evs = []
        prev_apply = list(deps)
        for g in range(4):
            t0 = 512 * g
            eld = dma("sy", xslab_u[:],
                      src_main_d[:, :, main_off + t0:main_off + t0 + 512],
                      prev_apply)
            grp = []
            for k in range(4):
                grp.append((lambda c, k=k: xslab_u[:, c, 128 * k:128 * k + 128],
                            [eld]))

            def dstf(k, ein, mu_ap, r_ap, t0=t0):
                last = None
                for c in range(8):
                    sv = xslab_u[:, c, 128 * k:128 * k + 128]
                    ee1 = tt("ve", tmps[:, c, 0:128], sv, mu_ap, ALU.subtract, [ein])
                    last = tt("ve", U[:, c, 1 + t0 + 128 * k:1 + t0 + 128 * k + 128],
                              tmps[:, c, 0:128], r_ap, ALU.mult, [ee1])
                return last
            res = ln_grp(grp, dstf)
            prev_apply = res
            all_evs += res

        def hsrc(c, k):
            return Uh[:, c, 128 * k:128 * k + 128]

        def dsth(k_in_grp, ein, mu_ap, r_ap, koff=0):
            k = k_in_grp + koff
            last = None
            for c in range(8):
                ee1 = tt("ve", tmps[:, c, 0:128], hsrc(c, k), mu_ap,
                         ALU.subtract, [ein])
                last = tt("ve", Uh[:, c, 128 * k:128 * k + 128], tmps[:, c, 0:128],
                          r_ap, ALU.mult, [ee1])
            return last
        grp = [(lambda c, k=k: hsrc(c, k), list(halo_deps)) for k in range(4)]
        r1 = ln_grp(grp, dsth)
        r2 = ln_grp([(lambda c: hsrc(c, 4), list(halo_deps) + r1)],
                    lambda kk, ein, m, r: dsth(kk, ein, m, r, koff=4))
        all_evs += r1 + r2
        ehc = cp("ve", U[:, :, 0:1], Uh[:, :, HALO - 1:HALO], r1 + r2)
        all_evs.append(ehc)
        return all_evs

    def exchange_big(idx, src_ap, src_deps, dst_deps):
        e1 = dma("gp", xch_in[idx], src_ap, src_deps)
        e2 = E("gp", lambda e, i=idx: e.collective_compute(
            "AllGather", ALU.bypass,
            replica_groups=[[0, 1], [2, 3], [4, 5], [6, 7]],
            ins=[xch_in[i][:]], outs=[xch_out[i][:]]), [e1], sem="cc", inc=1)
        e3 = dma("gp", Uh[:], xch_out[idx][0], [e2] + dst_deps)
        e4 = dma("gp", halo_a[:], xch_out[idx][1], [e2] + dst_deps)
        e5 = tss("ve", Uh[:], Uh[:], flags[:, 0:1], ALU.mult, [e3, ev_fl])
        e6 = stt("ve", Uh[:], halo_a[:], flags[:, 1:2], Uh[:], ALU.mult, ALU.add,
                 [e4, e5])
        return e6

    def exchange_small(idx, src_ap, src_deps, dst_ap, dst_deps):
        e0 = cp("ve", sxc[:], src_ap, src_deps)
        e1 = dma("gp", sch_in[idx], sxc[:], [e0])
        e2 = E("gp", lambda e, i=idx: e.collective_compute(
            "AllGather", ALU.bypass,
            replica_groups=[[0, 1], [2, 3], [4, 5], [6, 7]],
            ins=[sch_in[i][:]], outs=[sch_out[i][:]]), [e1], sem="cc", inc=1)
        e3 = dma("gp", sxa[:], sch_out[idx][0], [e2] + dst_deps)
        e4 = dma("gp", sxb[:], sch_out[idx][1], [e2])
        e5 = tss("ve", dst_ap, sxa[:], flags[:, 0:1], ALU.mult,
                 [e3, ev_fl] + dst_deps)
        e6 = stt("ve", dst_ap, sxb[:], flags[:, 1:2], dst_ap,
                 ALU.mult, ALU.add, [e4, e5])
        return e6

    def s_view(c, tau):
        base = 1 + 128 * tau - (1 if c >= 4 else 0)
        return U[:, c, base:base + 128]

    def sh_view(c, hi):
        base = HALO - 512 + 128 * hi - (1 if c >= 4 else 0)
        return Uh[:, c, base:base + 128]

    def load_scales(l):
        e1 = dma("sy", qs_rep[:], AP(qsv_d, 64 * l, [[0, 128], [1, 64]]))
        e1b = dma("sy", qsp_rep[:], AP(qsv_d, 64 * l + 32,
                                       [[0, 128], [-32, 2], [1, 32]]), [e1])
        e2 = dma("sy", ks_rep[:], AP(ksv_d, 64 * l, [[0, 128], [1, 64]]), [e1b])
        e2b = dma("sy", ksp_rep[:], AP(ksv_d, 64 * l + 32,
                                       [[0, 128], [-32, 2], [1, 32]]), [e2])
        return e2b

    def load_xpos_win(t0_tile, ev_scales, deps):
        ti0 = t0_tile + 5
        vbar = p.barrier(("ve", "te"))
        out = []
        for ti, nm in enumerate(("xAq", "xBq", "xAk", "xBk")):
            src_hi = AP(tabs_d[nm], ti0 * 128 * 64,
                        [[64, 128], [128 * 64, 4], [1, 64]])
            src_lo = AP(tabs_d[nm], 21 * 128 * 64 + ti0 * 128 * 64,
                        [[64, 128], [128 * 64, 4], [1, 64]])
            e0 = dma("gp", xtf[:, ti], src_hi, list(deps) + vbar)
            e1 = dma_acc("gp", xtf[:, ti], src_lo, [e0])
            fold = (qs_rep, qsp_rep, ks_rep, ksp_rep)[ti]
            fb = AP(fold[:], 0, [fold[:].ap[0], [0, 4], [1, 64]])
            out.append(tt("ve", xtf[:, ti], xtf[:, ti], fb, ALU.mult,
                          [e1, ev_scales]))
        return out

    qsq_guard = [[]]
    wk_guard = [[]]
    wo_state = {"ld": None, "g": {0: [], 1: []}}
    mlp_state = {"w1ld": None, "w2ld": None, "g": {0: [], 1: []}}
    aW_state = {"ld": None, "g": {0: [], 1: []}}
    awo_state = {"ld": None, "g": {0: [], 1: []}}

    def rotary(kind, src_ps, tau_in_win, tab_evs, deps):
        ti_A = 0 if kind == "q" else 2
        esq = act(qsq[:], src_ps, AF.Square, deps + qsq_guard[0])
        rn_dst = rnq[:, tau_in_win, :] if kind == "q" else rnk[:]
        er = red("ve", rn_dst, qsq[:].rearrange("p (h d) -> p h d", h=8),
                 ALU.add, [esq])
        qsq_guard[0] = [er]
        deps = list(deps) + [esq]
        er = tss("ve", rn_dst, rn_dst, 1e-24, ALU.max, [er])
        er = act(rn_dst, rn_dst, AF.Sqrt, [er])
        er = recip("ve", rn_dst, rn_dst, [er])
        Ai = xtf[:, ti_A, tau_in_win, :]
        Bi = xtf[:, ti_A + 1, tau_in_win, :]
        x3 = src_ps.rearrange("p (h d) -> p h d", h=8)
        Ab = AP(Ai, 0, [Ai.ap[0], [0, 8], [1, 64]])
        perm = AP(x3, 32, [x3.ap[0], x3.ap[1], [-32, 2], [1, 32]])
        Bb = AP(Bi, 0, [Bi.ap[0], [0, 8], [32, 2], [1, 32]])
        w0 = wk[:, 0, 0:512].rearrange("p (h d) -> p h d", h=8)
        w1_ = wk[:, 1, 0:512].rearrange("p (h s d) -> p h s d", h=8, s=2)
        e1 = tt("ve", w0, x3, Ab, ALU.mult, deps + tab_evs)
        e2 = tt("ve", w1_, perm, Bb, ALU.mult, deps + tab_evs)
        e3 = tt("ve", wk[:, 2, 0:512], wk[:, 0, 0:512], wk[:, 1, 0:512], ALU.add,
                [e1, e2] + wk_guard[0])
        return e3, er

    def emit_qkv_tau(l, tau_g, srcf, base_deps, want_q, tab_evs, bank_guards):
        slot = tau_g % 8
        piw = tau_g % 4
        res = {}
        kinds = (["k", "v", "q"] if want_q else ["k", "v"])
        for kind in kinds:
            dst = {"q": P3[:, 0:512], "k": P3[:, 512:1024], "v": P0[:, 0:512]}[kind]
            col0 = {"q": 0, "k": 512, "v": 1024}[kind]
            evm = None
            for c in range(8):
                evm = mm(dst, srcf(c), wqkv[:, c, col0:col0 + 512], c == 0, c == 7,
                         base_deps + (bank_guards if c == 0 else []))
            if kind == "v":
                res["v"] = cp("sc", vr[:, slot, :], dst, [evm])
            elif kind == "k":
                e3, er = rotary("k", dst, piw, tab_evs, [evm])
                ek = []
                for h in range(8):
                    ek.append(tss("ve", wk[:, 2, 64 * h:64 * h + 64],
                                  wk[:, 2, 64 * h:64 * h + 64], rnk[:, h:h + 1],
                                  ALU.mult, [e3, er]))
                evt = [tr(P1[:, 128 * cc:128 * cc + 128],
                          wk[:, 2, 128 * cc:128 * cc + 128], ek + bank_guards)
                       for cc in range(4)]
                res["k"] = cp("sc", kTr[:, :, slot, :],
                              P1[:, 0:512].rearrange("p (c t) -> p c t", c=4), evt)
                wk_guard[0] = [res["k"]]
            else:
                e3, er = rotary("q", dst, piw, tab_evs, [evm])
                res["q_rn"] = tss("ve", rnqn[:, piw, :], rnq[:, piw, :], -1.0,
                                  ALU.mult, [er])
                evt = [tr(P1[:, 128 * cc:128 * cc + 128],
                          wk[:, 2, 128 * cc:128 * cc + 128],
                          [e3, res.get("k")] + bank_guards)
                       for cc in range(4)]
                res["q"] = cp("sc", qTw[:, :, piw, :],
                              P1[:, 0:512].rearrange("p (c t) -> p c t", c=4), evt)
                wk_guard[0] = [res["q"]]
        return res

    def attn_qk(tau, h, kslots, Sp, deps):
        piw = tau % 4
        c, hf = h // 2, h % 2
        lq = qTw[64 * hf:64 * hf + 64, c, piw, :]
        evs = []
        for j, sl in enumerate(kslots):
            rk = kTr[64 * hf:64 * hf + 64, c, sl, :]
            evs.append(mm(Sp[:, 128 * j:128 * j + 128], lq, rk,
                          j == 0 or j == 4, False, deps))
        em0 = mm(Sp[:, 0:128], ident[:], maskU[:], False, True,
                 [evs[0], ev_mU, ev_id])
        em4 = mm(Sp[:, 512:640], ident[:], maskL[:, 0:128], False, True,
                 [evs[4], ev_mL])
        return evs + [em0, em4]

    def attn_soft(tau, h, Sp, qk_evs, pt_sel):
        piw = tau % 4
        prevN = 128 * (4 - piw)
        off_col = flags[:, 2 + tau:3 + tau]
        m0 = mx[:, 4 * pt_sel + 0:4 * pt_sel + 1]
        m1 = mx[:, 4 * pt_sel + 1:4 * pt_sel + 2]
        m2 = mx[:, 4 * pt_sel + 2:4 * pt_sel + 3]
        emx = red("ve", m0, Sp, ALU.max, qk_evs)
        ebp = stt("ve", m1, m0, rnqn[:, piw, h:h + 1], off_col,
                  ALU.mult, ALU.add, [emx, ev_fl])
        ebo = tss("ve", m2, m0, rnqn[:, piw, h:h + 1], ALU.mult, [emx])
        Pb = Pt[:, pt_sel, :]
        ee1 = E("sc", lambda e, o=Pb[:, 0:prevN], i=Sp[:, 0:prevN],
                b=m1, s=rnq[:, piw, h:h + 1]:
                e.activation(out=o, in_=i, func=AF.Exp, bias=b, scale=s), [ebp])
        ee2 = E("sc", lambda e, o=Pb[:, prevN:640], i=Sp[:, prevN:640],
                b=m2, s=rnq[:, piw, h:h + 1]:
                e.activation(out=o, in_=i, func=AF.Exp, bias=b, scale=s), [ebo])
        return [ee1, ee2]

    def attn_ptav(tau, h, kslots, pt_sel, exp_evs, pt_guard, o_start, o_guard):
        Pb = Pt[:, pt_sel, :]
        evt = [tr(P2[:, 128 * j:128 * j + 128], Pb[:, 128 * j:128 * j + 128],
                  exp_evs + pt_guard) for j in range(5)]
        epts = cp("sc", PTs[:], P2[:, 0:640], evt)
        eav = []
        for j, sl in enumerate(kslots):
            lp = PTs[:, 128 * j:128 * j + 128]
            gg = o_guard if j == 0 else []
            eav.append(mm(P3[:, 64 * h:64 * h + 64], lp,
                          vr[:, sl, 64 * h:64 * h + 64],
                          o_start and j == 0, h == 7 and j == 4, [epts] + gg))
            eav.append(mm(P3[:, 512 + h:513 + h], lp, onescol[:],
                          o_start and j == 0, h == 7 and j == 4,
                          [epts, ev_ones] + gg))
        return eav, epts

    def ffn_layer(l, lpos, ev_wq, u_deps, uh_deps):
        ev_sc = load_scales(l)
        ecol = cp("ve", U[:, :, 0:1], Uh[:, :, HALO - 1:HALO], uh_deps)
        u_deps = list(u_deps) + [ecol]
        wo_state["g"] = {0: wo_state["g"][0] + list(uh_deps),
                         1: wo_state["g"][1] + list(uh_deps)}
        tab_evs = load_xpos_win(-4, ev_sc, list(uh_deps))
        kv_prev = []
        bank_g = p.barrier(("ve", "sc"))
        for hi in range(4):
            rr = emit_qkv_tau(l, hi - 4, lambda c, hi=hi: sh_view(c, hi),
                              uh_deps + [ev_wq] + tab_evs, False, tab_evs, bank_g)
            kv_prev += [rr["k"], rr["v"]]
            bank_g = p.barrier(("ve", "sc"))

        pend_ev = None
        u_evs = []
        for w in range(4):
            tab_evs = load_xpos_win(4 * w, ev_sc, tab_evs)
            cur_guard = p.barrier(("ve", "sc"))
            kv_ev = []
            q_ev = []
            for i in range(4):
                tau = 4 * w + i
                dd = u_deps + [ev_wq] + tab_evs
                if pend_ev is not None and i == 0:
                    dd = dd + [pend_ev]
                rr = emit_qkv_tau(l, tau, lambda c, tau=tau: s_view(c, tau),
                                  dd, True, tab_evs, cur_guard)
                kv_ev += [rr["k"], rr["v"]]
                q_ev.append((rr["q"], rr["q_rn"]))
                cur_guard = p.barrier(("ve", "sc"))
            if pend_ev is not None:
                eu = tt("ve", U[:, :, 512 * w:512 * w + 1],
                        U[:, :, 512 * w:512 * w + 1], pend[:, :, 0:1], ALU.add,
                        [pend_ev] + kv_ev[-2:] + [q_ev[0][0]])
                u_evs.append(eu)
                pend_ev = None

            pairs = [(i, h) for i in range(4) for h in range(8)]
            state = {}
            qkv_done = kv_ev + [q for q, _ in q_ev] + [r for _, r in q_ev]
            o_evac = []

            def flush(idx_f, o_evac=o_evac, state=state):
                s = state[idx_f]
                og = [o_evac[-1]] if (o_evac and s["h"] == 0) else []
                eav, epts = attn_ptav(s["tau"], s["h"], s["kslots"],
                                      idx_f % 2, s["exp"],
                                      [state[idx_f - 2]["epts"]] if idx_f >= 2 else [],
                                      s["h"] == 0, og)
                s["av"] = eav
                s["epts"] = epts
                if s["h"] == 7:
                    erc = recip("ve", rcp[:], P3[:, 512:520], eav)
                    for hh in range(8):
                        o_evac.append(tss("ve",
                                          osb[:, s["i"], 64 * hh:64 * hh + 64],
                                          P3[:, 64 * hh:64 * hh + 64],
                                          rcp[:, hh:hh + 1], ALU.mult, [erc]))

            for idx, (i, h) in enumerate(pairs):
                tau = 4 * w + i
                kslots = [(4 * (w - 1) + b) % 8 for b in range(i, 4)] + \
                         [(4 * w + b) % 8 for b in range(0, i + 1)]
                Sp = (P0, P1)[idx % 2][:, 0:640]
                deps = qkv_done + kv_prev
                if idx >= 2:
                    deps = deps + state[idx - 2]["exp"]
                qk = attn_qk(tau, h, kslots, Sp, deps)
                ex = attn_soft(tau, h, Sp, qk, idx % 2)
                state[idx] = {"exp": ex, "kslots": kslots, "tau": tau, "h": h,
                              "i": i}
                if idx >= 1:
                    flush(idx - 1)
            flush(len(pairs) - 1)
            kv_prev = kv_ev

            bar = p.barrier(("ve", "sc"))
            evtr = []
            for i in range(4):
                for oc in range(4):
                    dd = o_evac + (bar if (i == 0 and oc == 0) else []) + \
                        (evtr[-1:] if evtr else [])
                    et = tr(P0[:, 0:128], osb[:, i, 128 * oc:128 * oc + 128], dd)
                    evtr.append(cp("sc", oTs[:, oc, 128 * i:128 * i + 128],
                                   P0[:, 0:128], [et]))
            bank_ev = {}
            base = 1 + 512 * w
            for dc in range(8):
                pps = (P1, P2)[(dc // 2) % 2][:, 512 * (dc % 2):512 * (dc % 2) + 512]
                evm = None
                for oc in range(4):
                    buf = wob[:, oc % 2, :]
                    ld = dma("sy", buf, wo_d[l, 128 * oc:128 * oc + 128,
                                             128 * dc:128 * dc + 128],
                             ([wo_state["ld"]] if wo_state["ld"] else []) +
                             wo_state["g"][oc % 2])
                    wo_state["ld"] = ld
                    dd = [ld] + evtr
                    if oc == 0 and dc - 4 in bank_ev:
                        dd.append(bank_ev[dc - 4])
                    if oc == 0 and dc < 4:
                        dd += bar
                    evm = mm(pps, buf, oTs[:, oc, :], oc == 0, oc == 3, dd)
                    wo_state["g"][oc % 2] = [evm]
                ep = cp("ve", pend[:, dc, 0:1], pps[:, 511:512], [evm])
                eu = tt("ve", U[:, dc, base:base + 511], U[:, dc, base:base + 511],
                        pps[:, 0:511], ALU.add, [evm, ep])
                bank_ev[dc] = eu
                u_evs.append(eu)
                pend_ev = eu
        eu = tt("ve", U[:, :, 2048:2049], U[:, :, 2048:2049], pend[:, :, 0:1],
                ALU.add, [pend_ev])
        u_evs.append(eu)
        attn_done = p.barrier()
        if DBG["stop"] == "l1a" and lpos == 0:
            e1 = dma("gp", dbg_d, U[:], u_evs + attn_done)
            E("gp", lambda e: e.memset(onescol[0:1, 0:1], 1.0), [e1])
            return None

        exch = exchange_small(lpos, U[:, :, 2048:2049], [eu], U[:, :, 0:1],
                              attn_done)
        if DBG["stop"] == "l1x" and lpos == 0:
            e1 = dma("gp", dbg_d, U[:], u_evs + attn_done + [exch])
            E("gp", lambda e: e.memset(onescol[0:1, 0:1], 1.0), [e1])
            return None
        eb1 = dma("sy", b1s[:], b1p_d[l])
        eb2 = dma("sy", b2s[:], b2v_d[l])
        elg = dma("sy", lngs[:], lng_d[l])
        elbc = dma("sy", lnbcol[:], lnbc_d[l])

        mlp_out = []
        c2_ev = [None] * (2 * NJ)
        first = True
        first_dma = True
        pending_from = {}
        order = [2, 3, 4, 5, 6, 7, 0, 1]
        prev_upd = attn_done
        for s in order:
            t0 = 256 * s
            dps = list(prev_upd)
            if s == 0:
                dps.append(exch)
            grp = [(lambda c, k=k, s=s: s_view(c, 2 * s + k), dps) for k in range(2)]
            applies = []

            def dstf(k, ein, mu_ap, r_ap, s=s, applies=applies):
                last = None
                for c in range(8):
                    sv = s_view(c, 2 * s + k)
                    ee1 = tt("ve", tmps[:, c, 0:128], sv, mu_ap, ALU.subtract, [ein])
                    last = stt("ve", s2n[:, c, 128 * k:128 * k + 128],
                               tmps[:, c, 0:128], lngs[:, c:c + 1], r_ap,
                               ALU.mult, ALU.mult, [ee1, elg])
                    applies.append(last)
                return last
            ln_grp(grp, dstf)
            s2n_ev = applies[-1]
            if DBG["stop"] == "l1mA" and lpos == 0:
                e1 = dma("gp", dbg2_d[:, :, 0:256], s2n[:], applies + p.barrier())
                E("gp", lambda e: e.memset(onescol[0:1, 0:1], 1.0), [e1])
                return None
            sprev = (s - 1) % 8
            if sprev in pending_from:
                pe, col = pending_from.pop(sprev)
                u_evs.append(cp("ve", U[:, :, col:col + 1],
                                mpend[:, :, sprev:sprev + 1],
                                [pe] + applies[-2:]))
            w2acc = [None] * 8
            pg_guard = []
            buf_guard = mlp_state["g"]
            for j in range(NJ):
                bsel = j % 2
                d0 = attn_done if first_dma else []
                first_dma = False
                e_w1 = dma("sy", w1b[:, bsel].rearrange("p c n -> p (c n)"),
                           AP(w1p_d, l * 1024 * 2 * HPAD + 256 * j,
                              [[2 * HPAD, 128], [128 * 2 * HPAD, 8], [1, 256]]),
                           ([mlp_state["w1ld"]] if mlp_state["w1ld"] else []) +
                           buf_guard[bsel] + d0)
                mlp_state["w1ld"] = e_w1
                e_w2 = dma("sy", w2b[:, bsel].rearrange("p c n -> p (c n)"),
                           AP(w2p_d, l * HPAD * 1024 + 128 * j * 1024,
                              [[1024, 128], [128, 8], [1, 128]]),
                           ([mlp_state["w2ld"]] if mlp_state["w2ld"] else []) +
                           buf_guard[bsel] + d0)
                mlp_state["w2ld"] = e_w2
                ae = ge = None
                for c in range(8):
                    gg = p.barrier(("ve", "sc")) if (c == 0 and j == 0) \
                        else (pg_guard if c == 0 else [])
                    ae = mm(P0[:, 0:256], w1b[:, bsel, c, 0:128], s2n[:, c, :],
                            c == 0, False, [e_w1, s2n_ev] + gg)
                    ge = mm(P0[:, 256:512], w1b[:, bsel, c, 128:256], s2n[:, c, :],
                            False, c == 7, [e_w1, s2n_ev])
                if first:
                    ca = cb = None
                    for c in range(8):
                        ca = mm(P3[:, 0:1], w1b[:, bsel, c, 0:128],
                                lnbcol[:, c:c + 1], c == 0, False,
                                [e_w1, elbc] + (c2_ev[2 * j - 1:2 * j] if j else []))
                        cb = mm(P3[:, 1:2], w1b[:, bsel, c, 128:256],
                                lnbcol[:, c:c + 1], False, c == 7, [e_w1, elbc])
                    c2_ev[2 * j] = tt("ve", c2s[:, 2 * j:2 * j + 1], P3[:, 0:1],
                                      b1s[:, 2 * j:2 * j + 1], ALU.add, [ca, cb, eb1])
                    c2_ev[2 * j + 1] = tt("ve", c2s[:, 2 * j + 1:2 * j + 2],
                                          P3[:, 1:2], b1s[:, 2 * j + 1:2 * j + 2],
                                          ALU.add, [cb, eb1])
                hb = j % 2
                eg = E("sc", lambda e, o=gel[:, hb, :], i=P0[:, 256:512],
                       b=c2s[:, 2 * j + 1:2 * j + 2]:
                       e.activation(out=o, in_=i, func=AF.Gelu, bias=b),
                       [ge, c2_ev[2 * j + 1]] + buf_guard[hb])
                eh = stt("ve", h2[:, hb, :], P0[:, 0:256], c2s[:, 2 * j:2 * j + 1],
                         gel[:, hb, :], ALU.add, ALU.mult,
                         [ae, eg, c2_ev[2 * j]] + buf_guard[hb])
                pg_guard = [eh, eg]
                if DBG["stop"] == "l1mB" and lpos == 0 and j == 1:
                    e1 = dma("gp", dbg2_d[:, 0:1, 0:256], h2[:, hb:hb + 1, :],
                             [eh] + p.barrier())
                    E("gp", lambda e: e.memset(onescol[0:1, 0:1], 1.0), [e1])
                    return None
                for dc in range(8):
                    pps = (P1, P2)[dc // 4][:, 256 * (dc % 4):256 * (dc % 4) + 256]
                    gg = []
                    if j == 0 and dc % 2 == 0:
                        gg = p.barrier(("ve", "sc"))
                    w2acc[dc] = mm(pps, w2b[:, bsel, dc, :], h2[:, hb, :],
                                   j == 0 and dc % 2 == 0,
                                   j == NJ - 1 and dc % 2 == 1,
                                   [e_w2, eh] + gg)
                buf_guard[bsel] = [w2acc[7], ae, ge, eh]
            first = False
            if DBG["stop"] == "l1mC" and lpos == 0:
                e1 = dma("gp", dbg2_d[:, 0:1, 0:256], h2[:, 0:1, :],
                         [w2acc[7]] + p.barrier())
                E("gp", lambda e: e.memset(onescol[0:1, 0:1], 1.0), [e1])
                return None
            upd = []
            for dc in range(8):
                pps = (P1, P2)[dc // 4][:, 256 * (dc % 4):256 * (dc % 4) + 256]
                sl = dc % 2
                slot_g = upd[-3:-2] if dc >= 2 else [pg_guard[0]]
                ecp = cp("sc", gel[:, sl, 0:256], pps,
                         [w2acc[dc | 1], eb2] + slot_g)
                ee = stt("ve", U[:, dc, 1 + t0:1 + t0 + 255],
                         gel[:, sl, 0:255], b2s[:, dc:dc + 1],
                         U[:, dc, 1 + t0:1 + t0 + 255], ALU.add, ALU.add,
                         [ecp] + (upd[-1:] if upd else []))
                ep = stt("ve", mpend[:, dc, s:s + 1], gel[:, sl, 255:256],
                         b2s[:, dc:dc + 1], U[:, dc, t0 + 256:t0 + 257],
                         ALU.add, ALU.add, [ecp, ee])
                upd += [ee, ep]
            pending_from[s] = (upd[-1], t0 + 256)
            mlp_out += upd
            u_evs += upd
            prev_upd = upd[-2:]
            if DBG["stop"] == "l1s" + str(s) and lpos == 0:
                e1 = dma("gp", dbg2_d, U[:, :, 513:513 + HALO],
                         upd + p.barrier())
                E("gp", lambda e: e.memset(onescol[0:1, 0:1], 1.0), [e1])
                return None
        for s, (pe, col) in list(pending_from.items()):
            mlp_out.append(cp("ve", U[:, :, col:col + 1], mpend[:, :, s:s + 1],
                              [pe]))
            pending_from.pop(s)
        return mlp_out


    ev_wq = None
    for c in range(8):
        ev_wq = dma("sy", wqkv[:, c, :], wqkv_d[0, 128 * c:128 * c + 128, :],
                    [ev_wq] if ev_wq else [])

    ev_xh = dma("sy", Uh[:], xin[:, :, 0:HALO])
    if DBG["stop"] and DBG["stop"].startswith("u0g"):
        ngrp = int(DBG["stop"][3:])
        prev_apply = []
        for g in range(ngrp):
            t0 = 512 * g
            eld = dma("sy", xslab_u[:], xin[:, :, HALO + t0:HALO + t0 + 512],
                      prev_apply)
            grp = [(lambda c, k=k: xslab_u[:, c, 128 * k:128 * k + 128], [eld])
                   for k in range(4)]

            def dstf(k, ein, mu_ap, r_ap, t0=t0):
                last = None
                for c in range(8):
                    sv = xslab_u[:, c, 128 * k:128 * k + 128]
                    ee1 = tt("ve", tmps[:, c, 0:128], sv, mu_ap, ALU.subtract, [ein])
                    last = tt("ve", U[:, c, 1 + t0 + 128 * k:1 + t0 + 128 * k + 128],
                              tmps[:, c, 0:128], r_ap, ALU.mult, [ee1])
                return last
            prev_apply = ln_grp(grp, dstf)
        e1 = dma("gp", dbg_d, U[:], prev_apply)
        E("gp", lambda e: e.memset(onescol[0:1, 0:1], 1.0), [e1])
        return ctx
    if DBG["stop"] == "pre":
        e1 = dma("gp", dbg_d, U[:], [ev_xh, ev_c, ev_lamn, ev_lnw, ev_lnb])
        E("gp", lambda e: e.memset(onescol[0:1, 0:1], 1.0), [e1])
        return ctx
    u0 = u0_pass(xin, HALO, [ev_xh], [])
    if DBG["stop"] == "u0":
        e1 = dma("gp", dbg_d, U[:], u0)
        e2 = dma("gp", dbg2_d, Uh[:], u0)
        E("gp", lambda e: e.memset(onescol[0:1, 0:1], 1.0), [e1, e2])
        return ctx

    l1 = ffn_layer(0, 0, ev_wq, u0, u0)
    if l1 is None:
        return ctx
    if DBG["stop"] == "l1":
        e1 = dma("gp", dbg_d, U[:], l1 + p.barrier())
        E("gp", lambda e: e.memset(onescol[0:1, 0:1], 1.0), [e1])
        return ctx

    bar = p.barrier()
    ev_wq2 = None
    for c in range(8):
        ev_wq2 = dma("sy", wqkv[:, c, :], wqkv_d[1, 128 * c:128 * c + 128, :],
                     ([ev_wq2] if ev_wq2 else []) + (bar if c == 0 else []))
    exch1 = exchange_big(0, U[:, :, 1409:2049], l1[-1:] + bar, bar)
    l2 = ffn_layer(1, 1, ev_wq2, l1, [exch1])
    if DBG["stop"] == "ffn1":
        e1 = dma("gp", dbg_d, U[:], l2 + p.barrier())
        E("gp", lambda e: e.memset(onescol[0:1, 0:1], 1.0), [e1])
        return ctx

    bar = p.barrier()
    xupd = {}
    prev_ee = []
    for sg in range(4):
        t0 = 512 * sg
        eld = dma("sy", xslab_x[:], xin[:, :, HALO + t0:HALO + t0 + 512],
                  bar + prev_ee)
        ee = []
        for c in range(8):
            ee.append(stt("ve", xq_x[:, c, :], U[:, c, 1 + t0:1 + t0 + 512],
                          cc_("half"), xslab_x[:, c, :], ALU.mult, ALU.add,
                          [eld, ev_c] + bar + ([xupd[sg - 1]] if sg else [])))
        xupd[sg] = dma("gp", xpost_d[:, :, t0:t0 + 512], xq_x[:], ee)
        prev_ee = ee[-1:]
    if DBG["stop"] == "xupd":
        e1 = dma("gp", dbg_d[:, :, 0:T_LOC], xpost_d, list(xupd.values()))
        E("gp", lambda e: e.memset(onescol[0:1, 0:1], 1.0), [e1])
        return ctx

    bar_i = p.barrier()
    ev_itb = []
    for ti, nm in enumerate(("iAq", "iBq", "iA", "iB")):
        e0 = dma("gp", itb[:, ti], AP(tabs_d[nm], 0,
                                      [[64, 128], [128 * 64, 16], [1, 64]]),
                 bar_i + list(xupd.values()))
        ev_itb.append(dma_acc("gp", itb[:, ti],
                              AP(tabs_d[nm], 16 * 128 * 64,
                                 [[64, 128], [128 * 64, 16], [1, 64]]), [e0]))

    attn_x = {}
    prev_x2 = []
    prev_xw = None
    for so, sg in enumerate([2, 3, 0, 1]):
        t0 = 512 * sg
        bar0 = p.barrier(("ve", "sc")) + ([prev_xw] if prev_xw else [])
        eld = dma("sy", xslab_a[:], xpost_d[:, :, t0:t0 + 512],
                  [xupd[sg], prev_xw] + prev_x2)
        happ = []
        grp = [(lambda c, k=k: xslab_a[:, c, 128 * k:128 * k + 128],
                [eld] + bar0) for k in range(4)]

        def dsth(k, ein, mu_ap, r_ap, happ=happ):
            last = None
            for c in range(8):
                sv = xslab_a[:, c, 128 * k:128 * k + 128]
                ee1 = tt("ve", tmps[:, c, 0:128], sv, mu_ap, ALU.subtract, [ein])
                last = tt("ve", xq_a[:, c, 128 * k:128 * k + 128],
                          tmps[:, c, 0:128], r_ap, ALU.mult, [ee1])
                happ.append(last)
            return last
        ln_grp(grp, dsth)
        h_ev = happ[-1]

        rot_evs = []
        v_evs = []
        p3_cons = []
        bar1 = p.barrier(("ve", "sc"))
        awb_g = aW_state["g"]
        for wi, dst_t in enumerate((aq, ak, av)):
            for hf in range(2):
                bsel = (wi * 2 + hf) % 2
                eW = dma("sy", aWb[:, bsel].rearrange("p c n -> p (c n)"),
                         AP(aW_d, wi * 1024 * 1024 + 512 * hf,
                            [[1024, 128], [128 * 1024, 8], [1, 512]]),
                         ([aW_state["ld"]] if aW_state["ld"] else []) +
                         awb_g[bsel] +
                         ([prev_xw] if (prev_xw and wi == 0 and hf == 0) else []))
                aW_state["ld"] = eW
                for k in range(4):
                    evm = None
                    for c in range(8):
                        gg = bar1 if (wi == 0 and hf == 0 and k == 0 and c == 0) \
                            else (p3_cons if c == 0 else [])
                        evm = mm(P3[:, 0:512], xq_a[:, c, 128 * k:128 * k + 128],
                                 aWb[:, bsel, c, :], c == 0, c == 7,
                                 [eW, h_ev] + gg)
                    awb_g[bsel] = [evm]
                    dcols = dst_t[:, k, 512 * hf:512 * hf + 512]
                    if wi == 2:
                        v_evs.append(cp("sc", dcols, P3[:, 0:512], [evm]))
                        p3_cons = v_evs[-1:]
                    else:
                        tA = itb[:, 0 if wi == 0 else 2, 4 * sg + k, :]
                        tB = itb[:, 1 if wi == 0 else 3, 4 * sg + k, :]
                        x3 = P3[:, 0:512].rearrange("p (h d) -> p h d", h=8)
                        Ab = AP(tA, 0, [tA.ap[0], [0, 8], [1, 64]])
                        perm = AP(x3, 1, [x3.ap[0], x3.ap[1], [2, 32], [-1, 2]])
                        Bb = AP(tB, 0, [tB.ap[0], [0, 8], [2, 32], [1, 2]])
                        e1 = tt("ve", awk[:, 0, :].rearrange(
                            "p (h d) -> p h d", h=8), x3, Ab, ALU.mult,
                            [evm] + ev_itb + rot_evs[-1:])
                        e2 = tt("ve", awk[:, 1, :].rearrange(
                            "p (h s d) -> p h s d", h=8, s=32), perm, Bb, ALU.mult,
                            [evm] + ev_itb)
                        rot_evs.append(tt("ve", dcols, awk[:, 0, :],
                                          awk[:, 1, :], ALU.add, [e1, e2]))
                        p3_cons = rot_evs[-1:]
        qkv_ev = rot_evs + v_evs

        aT_evs = []
        ascr_g = []
        for k in range(4):
            q3 = aq[:, k, :]
            k3 = ak[:, k, :]
            sc_ev = {}
            for gi, off in ((0, 0), (1, 64)):
                qa = AP(q3, off, [q3.ap[0], [0, 8], [128, 8], [1, 64]])
                ka = AP(k3, off, [k3.ap[0], [128, 8], [0, 8], [1, 64]])
                e1 = tt("ve", ascr[:].rearrange("p (m n d) -> p m n d", m=8, n=8),
                        qa, ka, ALU.mult, qkv_ev + ascr_g)
                e2 = red("ve", (aS1, aS2)[gi][:].rearrange("p (m n) -> p m n", m=8),
                         ascr[:].rearrange("p (m n d) -> p m n d", m=8, n=8),
                         ALU.add, [e1])
                sc_ev[gi] = e2
                ascr_g = [e2]
            aw_evs = []
            for gi, (S, A) in enumerate(((aS1, aw1), (aS2, aw2))):
                mi = AP(S[:], 0, [S[:].ap[0], [1, 8], [8, 8]])
                em = red("ve", awm[:], mi, ALU.max, [sc_ev[gi]])
                mb = AP(awm[:], 0, [awm[:].ap[0], [0, 8], [1, 8]])
                es = tt("ve", A[:].rearrange("p (m n) -> p m n", m=8),
                        S[:].rearrange("p (m n) -> p m n", m=8), mb,
                        ALU.subtract, [em])
                ee = act(A[:], A[:], AF.Exp, [es])
                e2 = red("ve", awsum[:], AP(A[:], 0, [A[:].ap[0], [1, 8], [8, 8]]),
                         ALU.add, [ee])
                e3 = recip("ve", awsum[:], awsum[:], [e2])
                sb_ = AP(awsum[:], 0, [awsum[:].ap[0], [0, 8], [1, 8]])
                aw_evs.append(tt("ve", A[:].rearrange("p (m n) -> p m n", m=8),
                                 A[:].rearrange("p (m n) -> p m n", m=8), sb_,
                                 ALU.mult, [e3, ee]))
            eawc = stt("ve", aw2[:], aw2[:], lamneg[:, 0:1], aw1[:], ALU.mult,
                       ALU.add, [ev_lamn] + aw_evs)
            v3 = av[:, k, :]
            av_evs = []
            for n in range(8):
                va = AP(v3, 0, [v3.ap[0], [1, 128], [128, 8]])
                ab = AP(aw2[:], n, [aw2[:].ap[0], [0, 128], [8, 8]])
                e1 = tt("ve", ascr[:, 0:1024].rearrange("p (e m) -> p e m", e=128),
                        va, ab, ALU.mult, [eawc] + v_evs + ascr_g)
                rr = red("ve", aout[:, 128 * n:128 * n + 128],
                         ascr[:, 0:1024].rearrange("p (e m) -> p e m", e=128),
                         ALU.add, [e1])
                av_evs.append(rr)
                ascr_g = [rr]
            esq = act(ascr[:, 0:1024], aout[:], AF.Square, av_evs + ascr_g)
            er = red("ve", ss8[:], ascr[:, 0:1024].rearrange("p (n e) -> p n e", n=8),
                     ALU.add, [esq])
            ascr_g = [er]
            er = tss("ve", ss8[:], ss8[:], cc_("inv128"), ALU.mult, [er, ev_c])
            er = tss("ve", ss8[:], ss8[:], cc_("rms_eps"), ALU.add, [er])
            er = act(ss8[:], ss8[:], AF.Sqrt, [er])
            er = recip("ve", ss8[:], ss8[:], [er])
            apv = []
            for n in range(8):
                apv.append(tss("ve", aout[:, 128 * n:128 * n + 128],
                               aout[:, 128 * n:128 * n + 128], ss8[:, n:n + 1],
                               ALU.mult, [er]))
            lwb = AP(lnw_rep[:], 0, [lnw_rep[:].ap[0], [0, 8], [1, 128]])
            lbb = AP(lnb_rep[:], 0, [lnb_rep[:].ap[0], [0, 8], [1, 128]])
            ea1 = tt("ve", aout[:].rearrange("p (n e) -> p n e", n=8),
                     aout[:].rearrange("p (n e) -> p n e", n=8), lwb, ALU.mult,
                     apv + [ev_lnw])
            ea2 = tt("ve", aout[:].rearrange("p (n e) -> p n e", n=8),
                     aout[:].rearrange("p (n e) -> p n e", n=8), lbb, ALU.add,
                     [ea1, ev_lnb])
            for ac in range(8):
                et = tr(P0[:, 0:128], aout[:, 128 * ac:128 * ac + 128],
                        [ea2] + (aT_evs[-1:] if aT_evs else []))
                aT_evs.append(cp("sc", aTs[:, ac, 128 * k:128 * k + 128],
                                 P0[:, 0:128], [et]))

        x2 = []
        bank_ev = {}
        awo_g = awo_state["g"]
        for dc in range(8):
            pps = (P1, P2)[(dc // 2) % 2][:, 512 * (dc % 2):512 * (dc % 2) + 512]
            evm = None
            for ac in range(8):
                buf = aWob[:, ac % 2, :]
                ld = dma("sy", buf, aW_d[3, 128 * ac:128 * ac + 128,
                                         128 * dc:128 * dc + 128],
                         ([awo_state["ld"]] if awo_state["ld"] else []) +
                         awo_g[ac % 2])
                awo_state["ld"] = ld
                dd = [ld] + aT_evs
                if ac == 0 and dc - 4 in bank_ev:
                    dd.append(bank_ev[dc - 4])
                if ac == 0 and dc == 0:
                    dd += p.barrier(("ve", "sc"))
                evm = mm(pps, buf, aTs[:, ac, :], ac == 0, ac == 7, dd)
                awo_g[ac % 2] = [evm]
            e = tt("ve", xslab_a[:, dc, :], xslab_a[:, dc, :], pps, ALU.add,
                   [evm, eld])
            bank_ev[dc] = e
            x2.append(e)
        attn_x[sg] = dma("gp", xpost_d[:, :, t0:t0 + 512], xslab_a[:], x2)
        prev_x2 = x2[-1:]
        prev_xw = attn_x[sg]
        if sg == 3:
            eread = dma("gp", xch_in[1], xpost_d[:, :, T_LOC - HALO:T_LOC],
                        [attn_x[2], attn_x[3]])
            ecc2 = E("gp", lambda e: e.collective_compute(
                "AllGather", ALU.bypass,
                replica_groups=[[0, 1], [2, 3], [4, 5], [6, 7]],
                ins=[xch_in[1][:]], outs=[xch_out[1][:]]),
                [eread], sem="cc", inc=1)

    bar = p.barrier()
    e3 = dma("gp", Uh[:], xch_out[1][0], [ecc2] + bar)
    e4 = dma("gp", halo_a[:], xch_out[1][1], [ecc2, attn_x[0], attn_x[1]])
    e5 = tss("ve", Uh[:], Uh[:], flags[:, 0:1], ALU.mult, [e3, ev_fl] + bar)
    exch2 = stt("ve", Uh[:], halo_a[:], flags[:, 1:2], Uh[:], ALU.mult, ALU.add,
                [e4, e5])

    if DBG["stop"] == "attn":
        e1 = dma("gp", dbg_d[:, :, 0:T_LOC], xpost_d, list(attn_x.values()))
        E("gp", lambda e: e.memset(onescol[0:1, 0:1], 1.0), [e1])
        return ctx

    ev_wq = None
    for c in range(8):
        ev_wq = dma("sy", wqkv[:, c, :], wqkv_d[0, 128 * c:128 * c + 128, :],
                    ([ev_wq] if ev_wq else []) + (bar if c == 0 else []))
    u0b = u0_pass(xpost_d, 0, [exch2], list(attn_x.values()) + bar + [exch2])
    if DBG["stop"] == "u0b":
        e1 = dma("gp", dbg_d, U[:], u0b)
        e2 = dma("gp", dbg2_d, Uh[:], u0b)
        E("gp", lambda e: e.memset(onescol[0:1, 0:1], 1.0), [e1, e2])
        return ctx
    l1b = ffn_layer(0, 2, ev_wq, u0b, u0b)
    bar = p.barrier()
    ev_wq2 = None
    for c in range(8):
        ev_wq2 = dma("sy", wqkv[:, c, :], wqkv_d[1, 128 * c:128 * c + 128, :],
                     ([ev_wq2] if ev_wq2 else []) + (bar if c == 0 else []))
    exch3 = exchange_big(2, U[:, :, 1409:2049], l1b[-1:] + bar, bar)
    l2b = ffn_layer(1, 3, ev_wq2, l1b, [exch3])

    bar = p.barrier()
    outs = []
    prev_ee = []
    for sg in range(4):
        t0 = 512 * sg
        eld = dma("sy", xslab_x[:], xpost_d[:, :, t0:t0 + 512], bar + prev_ee)
        ee = []
        for c in range(8):
            ee.append(stt("ve", xq_x[:, c, :], U[:, c, 1 + t0:1 + t0 + 512],
                          cc_("half"), xslab_x[:, c, :], ALU.mult, ALU.add,
                          [eld, ev_c] + bar + ([outs[-1]] if outs else [])))
        outs.append(dma("gp", out_d[:, :, t0:t0 + 512], xq_x[:], ee))
        prev_ee = ee[-1:]
    E("gp", lambda e: e.memset(onescol[0:1, 0:1], 1.0), outs)
    return ctx



def _build_emit(nc):
    p = Prog()
    ctx = build(nc, p)
    p.finalize()
    sem_ctx, sems = [], {}
    for s in p.sem_names():
        h = nc.semaphore(s)
        sem_ctx.append(h)
        sems[s] = h.__enter__()
    return p, ctx, sem_ctx, sems


def _run(in_maps):
    nc = bass.Bass("TRN2", target_bir_lowering=False)
    p, ctx, sem_ctx, sems = _build_emit(nc)
    with nc.Block() as block:
        def mk(ename):
            def body(eng):
                for st in p.steps[ename]:
                    for (s, v) in st.waits:
                        eng.wait_ge(sems[s], v)
                    inst = st.fn(eng)
                    if st.val is not None:
                        inst.then_inc(sems[st.sem], st.inc)
            return body
        block.tensor(mk("te"))
        block.vector(mk("ve"))
        block.scalar(mk("sc"))
        block.sync(mk("sy"))
        block.gpsimd(mk("gp"))
    for t in reversed(ctx):
        t.__exit__(None, None, None)
    for h in reversed(sem_ctx):
        h.__exit__(None, None, None)
    return run_bass_kernel_spmd(nc, in_maps, core_ids=list(range(8)))


def kernel(**inputs):
    x = np.asarray(inputs["L0_attn_x"])
    W = _prep_shared({k: np.asarray(v) for k, v in inputs.items()})
    in_maps = [_make_core_inputs(c, x, W) for c in range(8)]
    res = _run(in_maps)
    out = np.zeros((4, 4096, D), np.float32)
    for c in range(8):
        seq, half = c // 2, c % 2
        o = res.results[c]["out"]
        out[seq, half * T_LOC:(half + 1) * T_LOC] = (
            o.transpose(2, 1, 0).reshape(T_LOC, D))
    return out
